# version 05 (final)
# baseline (speedup 1.0000x reference)
import numpy as np
import ml_dtypes
import concourse.bass as bass
import concourse.mybir as mybir
from concourse.bass_types import AP
from concourse.bass_utils import run_bass_kernel_spmd
from concourse import dve_ops
from concourse.dve_op import DveOp
from concourse.dve_spec import Spec, Src0, Src1, C0, C1, C2, Zero, One, lower as dve_lower
from concourse.dve_table_gen import dve_ver_for

F32 = mybir.dt.float32
BF16 = mybir.dt.bfloat16
AF = mybir.ActivationFunctionType
OP = mybir.AluOpType
AX = mybir.AxisListType

_VER = dve_ver_for("TRN2")

def _f64(x):
    return np.float64 if x.dtype == np.float64 else np.float32

def _mk_op(name, spec, reference, row):
    for o in dve_ops.OPS:
        if o.name == name:
            return o
    sha = dve_lower(spec, ver=_VER, perf_en=False, name=name).sha(_VER)
    op = DveOp(name, spec, subdim=False, uops_sha={_VER: sha}, reference=reference)
    dve_ops.OPS.append(op)
    dve_ops._SUB_OPCODE_FOR_NAME[name] = row
    dve_ops.CUSTOM_DVE_SPECS[name] = spec
    dve_ops.CUSTOM_DVE_REFS[name] = reference
    return op

_t1 = Src0 >= C0
_t2 = Src0 >= C1
_bodyA = (Src0 + Src0) * ((C2 - (_t1 + _t1)) - _t2)
def _refA(in0, in1, s0, s1, imm2):
    f = _f64(in0); a = in0.astype(f)
    return ((a + a) * (imm2 - 2.0 * (a >= s0) - 1.0 * (a >= s1))).astype(f)
_c2n = One + One
_c4n = _c2n + _c2n
_r1n = Src0 - (Src0 >= C0) * C1
_bodyB = _r1n - (_r1n >= C2) * _c4n
def _refB(in0, in1, s0, s1, imm2):
    f = _f64(in0); u = in0.astype(f)
    r1 = u - (u >= s0) * s1
    return (r1 - (r1 >= imm2) * 4.0).astype(f)
_r3n = Src0 - (Src0 >= C0) * _c2n
_r4n = _r3n - (_r3n >= C1)
_bodyC = Src1 - _r4n
def _refC(in0, in1, s0, s1, imm2):
    f = _f64(in0); r2 = in0.astype(f); u = in1.astype(f)
    r3 = r2 - 2.0 * (r2 >= s0)
    r4 = r3 - 1.0 * (r3 >= s1)
    return (u - r4).astype(f)
_bodyD = Src1 * (C2 * ((One + (Src0 >= C0)) * (One + (Src0 >= C1))))
def _refD(in0, in1, s0, s1, imm2):
    f = _f64(in0); av = in0.astype(f); n = in1.astype(f)
    return (n * (imm2 * (1.0 + (av >= s0)) * (1.0 + (av >= s1)))).astype(f)

OPA = _mk_op("ATH_QA", Spec(body=_bodyA), _refA, 1)
OPB = _mk_op("ATH_QB", Spec(body=_bodyB), _refB, 2)
OPC = _mk_op("ATH_QC", Spec(body=_bodyC), _refC, 3)
OPD = _mk_op("ATH_QD", Spec(body=_bodyD), _refD, 4)

TH1, TH2 = 1.9375, 3.875

ENGS = ("sp", "act", "pe", "dve", "pool")
_BLOCK_ATTR = {"sp": "sync", "act": "scalar", "pe": "tensor",
               "dve": "vector", "pool": "gpsimd"}

class Sched:
    def __init__(self):
        self.plan = {e: [] for e in ENGS}
        self.cnt = {e: 0 for e in ENGS}
        self.dcnt = {e: 0 for e in ENGS}

    def _waits(self, eng, deps, kind, self_wait=0):
        waits = {}
        for tok in deps:
            if tok is None:
                continue
            if isinstance(tok, (list, tuple)) and tok and isinstance(tok[0], (list, tuple)):
                raise TypeError("nested dep list")
            k, e, c = tok
            if k == "s" and e == eng:
                continue
            key = (k, e)
            if waits.get(key, 0) < c:
                waits[key] = c
        if self_wait:
            waits[("s", eng)] = self_wait
        return waits

    def step(self, eng, fn, deps=(), self_wait=0):
        waits = self._waits(eng, deps, "s", self_wait=self_wait)
        self.plan[eng].append((waits, fn, "s"))
        self.cnt[eng] += 1
        return ("s", eng, self.cnt[eng])

    def dma(self, eng, fn, deps=()):
        waits = self._waits(eng, deps, "d")
        self.plan[eng].append((waits, fn, "d"))
        self.dcnt[eng] += 1
        return ("d", eng, self.dcnt[eng])

    def emit(self, block, sems, dsems):
        for ename in ENGS:
            items = self.plan[ename]
            def make_body(items=items, ename=ename):
                def body(eng):
                    marks = {}
                    for waits, fn, kind in items:
                        for (k, e), c in sorted(waits.items()):
                            if marks.get((k, e), 0) >= c:
                                continue
                            marks[(k, e)] = c
                            if k == "s":
                                eng.wait_ge(sems[e], c)
                            else:
                                eng.wait_ge(dsems[e], 16 * c)
                        last = fn(eng)
                        if isinstance(last, list):
                            last = last[-1]
                        if kind == "s":
                            last.then_inc(sems[ename], 1)
                        else:
                            last.then_inc(dsems[ename], 16)
                return body
            getattr(block, _BLOCK_ATTR[ename])(make_body())


class Res:
    def __init__(self):
        self.w = None
        self.readers = []
    def wdeps(self):
        return ([self.w] if self.w else []) + self.readers
    def rdeps(self):
        return [self.w] if self.w else []
    def wrote(self, t):
        self.w = t
        self.readers = []
    def read(self, t):
        self.readers.append(t)


SEG, D, CH, NCH_SC = 4096, 1024, 128, 4
SC = CH * NCH_SC
NSC = SEG // SC
WARM = 4
SEGX = SEG + WARM * CH
NCHUNK = SEGX // CH
NSCT = NCHUNK // NCH_SC
XRING = 4
EPS = 1e-5


STAGE = 5
SUB = 9
SCLIM = 99

def build_program(stage=None, sub=None, sclim=None, nsc=None, dumps=(), chlim=4):
    stage = STAGE if stage is None else stage
    sub = SUB if sub is None else sub
    sclim = SCLIM if sclim is None else sclim
    nsc = NSCT if nsc is None else nsc
    nc = bass.Bass("TRN2", target_bir_lowering=False)

    x_in = nc.dram_tensor("x_in", [SEGX, D], F32, kind="ExternalInput").ap()
    wq_d = nc.dram_tensor("wq", [D, 512], F32, kind="ExternalInput").ap()
    wk_d = nc.dram_tensor("wk", [D, 512], F32, kind="ExternalInput").ap()
    wv_d = nc.dram_tensor("wv", [D, 512], F32, kind="ExternalInput").ap()
    wa_d = nc.dram_tensor("wa", [D, 512], F32, kind="ExternalInput").ap()
    ba_d = nc.dram_tensor("ba", [512], F32, kind="ExternalInput").ap()
    wo_d = nc.dram_tensor("wo", [512, D], F32, kind="ExternalInput").ap()
    gamma_d = nc.dram_tensor("gamma", [D], F32, kind="ExternalInput").ap()
    beta_d = nc.dram_tensor("beta", [D], F32, kind="ExternalInput").ap()
    wd_d = nc.dram_tensor("wd", [D, D], F32, kind="ExternalInput").ap()
    fb_d = nc.dram_tensor("fbias", [D], F32, kind="ExternalInput").ap()
    ident_d = nc.dram_tensor("c_ident", [128, 128], BF16, kind="ExternalInput").ap()
    ones_d = nc.dram_tensor("c_ones", [1, 128], BF16, kind="ExternalInput").ap()
    tri4_d = nc.dram_tensor("c_tri4", [128, 512], BF16, kind="ExternalInput").ap()
    bd_d = nc.dram_tensor("c_bd", [128, 128], BF16, kind="ExternalInput").ap()
    cv_d = nc.dram_tensor("c_vals", [128, 4], F32, kind="ExternalInput").ap()
    wmk_d = nc.dram_tensor("wmk", [128, 1], F32, kind="ExternalInput").ap()
    out_d = nc.dram_tensor("out", [SEG, D], F32, kind="ExternalOutput").ap()

    x_r = x_in.rearrange("(n p) d -> n p d", p=128)
    out_r = out_d.rearrange("(n p) d -> n p d", p=128)

    A = lambda name, shape, dt=BF16: nc.alloc_sbuf_tensor(name, shape, dt)

    c_ident = A("c_identt", [128, 128]); c_ones = A("c_onest", [1, 128])
    c_tri4 = A("c_tri4t", [128, 512]); c_bd = A("c_bdt", [128, 128])
    c_vals = A("c_valst", [128, 4], F32)
    wmkt = A("wmkt", [128, 1], F32)
    gamma_pp = A("gamma_pp", [128, 8], F32)
    beta_pp = A("beta_pp", [128, 8], F32)
    ba_pp = A("ba_pp", [128, 4], F32)
    wqb = A("wqb", [128, 4096]); wkb = A("wkb", [128, 4096])
    wab = A("wab", [128, 4096]); wvb = A("wvb", [128, 4096])
    wdT = A("wdT", [128, 8192])
    wob = A("wob", [128, 4096])
    bq_pp = A("bq_pp", [128, 4], F32); bk_pp = A("bk_pp", [128, 4], F32)
    babias_pp = A("babias_pp", [128, 4], F32)
    bv_row = A("bv_row", [1, 512]); fb_row = A("fb_row", [1, 1024])
    sbf = A("sbf", [128, 512])
    sbf32 = A("sbf32", [128, 512], F32)

    x_t = A("x_t", [128, XRING * D], F32)
    xnbf = A("xnbf", [128, 4 * D])
    xn2a = [A(f"xn2a{p}", [128, 4096]) for p in range(2)]
    xn2aF = [A(f"xn2aF{p}", [128, 4096]) for p in range(2)]
    stagv = [xn2aF[0].ap().bitcast(F32), xn2aF[1].ap().bitcast(F32)]
    _xdc = A("xdc0", [128, 1024]); xdc = [_xdc, _xdc]
    xdF = [A(f"xdF{p}", [128, 1024]) for p in range(2)]
    _la1 = A("la0", [128, 512], F32); la_s = [_la1, _la1]
    _lx = A("Lx0t", [128, 513], F32); Lx = [_lx, _lx]
    qtF = [A(f"qtF{p}", [128, 2048]) for p in range(2)]
    ktF = [A(f"ktF{p}", [128, 2048]) for p in range(2)]
    kraw = [A(f"kraw{p}", [128, 512]) for p in range(2)]
    ksqF = [A(f"ksqF{p}", [128, 512]) for p in range(2)]
    ktok = [A(f"ktok{p}", [128, 2048]) for p in range(2)]
    vtok = [A(f"vtok{p}", [128, 512]) for p in range(2)]
    vpr = [A(f"vpr{p}", [128, 512]) for p in range(2)]
    _e1s = A("E1s0", [128, 512], F32); E1s = [_e1s, _e1s]
    e1k = [A(f"e1k{p}", [128, 512]) for p in range(2)]
    _ek1 = A("ekarg0", [128, 512], F32); ekarg = [_ek1, _ek1]
    ECt = [A(f"ECt{p}", [128, 16], F32) for p in range(2)]
    _aev = A("Aev0", [128, 512]); Aev = [_aev, _aev]
    _aod = A("Aod0", [128, 512]); Aod = [_aod, _aod]
    osb = [A(f"osb{p}", [128, 512]) for p in range(2)]
    _gsb = A("gsb0", [128, 1024]); gsb = [_gsb, _gsb]
    wsb = [A(f"wsb{p}", [128, 1024]) for p in range(2)]
    outb = [A(f"outb{p}", [128, 1024], F32) for p in range(2)]
    _qav = A("q_av0", [128, 1024], F32); q_av = [_qav, _qav]
    _qsg = A("q_sg0", [128, 1024]); q_sg = [_qsg, _qsg]
    _qt1 = A("q_t10", [128, 1024], F32); q_t1 = [_qt1, _qt1]
    _qt4 = A("q_t40", [128, 1024], F32); q_t4 = [_qt4, _qt4]
    _qt5 = A("q_t50", [128, 1024]); q_t5 = [_qt5, _qt5]
    bmax = [A(f"bmax{p}", [128, 32], F32) for p in range(2)]
    sclf = [A(f"sclf{p}", [128, 32], F32) for p in range(2)]
    sclb = [A(f"sclb{p}", [128, 32]) for p in range(2)]
    rcpb = [A(f"rcpb{p}", [128, 32]) for p in range(2)]
    bn6 = [A(f"bn6{p}", [128, 12], F32) for p in range(2)]
    mv = [A(f"mv{p}", [128, 4], F32) for p in range(2)]
    sml = [A(f"sml{p}", [128, 16], F32) for p in range(4)]
    rkt = [A(f"rkt{p}", [128, 32], F32) for p in range(2)]
    rkf = [A(f"rkf{p}", [128, 8], F32) for p in range(2)]
    nlc = [A(f"nlc{p}", [128, 4], F32) for p in range(2)]

    PBK = [nc.alloc_psum_tensor(f"pb{i}", [128, 512], F32) for i in range(8)]
    pq, pk, pa, pv, pA, pB_, po, pt = PBK
    pA_bf = pA.ap().bitcast(BF16)
    pB_bf = pB_.ap().bitcast(BF16)

    nc.const_aps.register(F32, EPS, c_vals.ap()[:, 0:1])

    S = Sched()
    R = {}
    def res(*key):
        if key not in R:
            R[key] = Res()
        return R[key]

    last_small = {e: (None, None, set()) for e in ENGS}
    def rstep(eng, fn, reads=(), writes=(), extra=(), small=False):
        deps = list(extra)
        for r in reads:
            deps += r.rdeps()
        for r in writes:
            deps += r.wdeps()
        ptok, psmall, pwrites = last_small[eng]
        self_wait = 0
        if psmall and (set(id(r) for r in reads) & pwrites
                       or set(id(r) for r in writes) & pwrites):
            self_wait = ptok[2]
        t = S.step(eng, fn, deps=deps, self_wait=self_wait)
        for r in reads:
            r.read(t)
        for r in writes:
            r.wrote(t)
        last_small[eng] = (t, small, set(id(r) for r in writes))
        return t

    def rdma(eng, fn, reads=(), writes=(), extra=()):
        deps = list(extra)
        for r in reads:
            deps += r.rdeps()
        for r in writes:
            deps += r.wdeps()
        t = S.dma(eng, fn, deps=deps)
        for r in reads:
            r.read(t)
        for r in writes:
            r.wrote(t)
        return t

    rPB = [res("pb", i) for i in range(8)]
    rpq, rpk, rpa, rpv, rpA, rpB, rpo, rpt = rPB
    r_stag = [res("stag", i) for i in range(2)]
    r_x = [res("x", i) for i in range(XRING)]
    r_xnbf = [res("xnbf", i) for i in range(4)]

    t_cv = rdma("sp", lambda e: e.dma_start(out=c_vals[:], in_=cv_d))
    t_wmk = rdma("sp", lambda e: e.dma_start(out=wmkt[:], in_=wmk_d), writes=[res("wmk")])
    t_id = rdma("sp", lambda e: e.dma_start(out=c_ident[:], in_=ident_d))
    t_on = rdma("sp", lambda e: e.dma_start(out=c_ones[:], in_=ones_d))
    t_tri = rdma("sp", lambda e: e.dma_start(out=c_tri4[:], in_=tri4_d))
    t_bd = rdma("sp", lambda e: e.dma_start(out=c_bd[:], in_=bd_d))
    t_gm = rdma("sp", lambda e: e.dma_start(
        out=gamma_pp[:], allow_slow_non_contiguous=True, in_=gamma_d.rearrange("(k p) -> p k", p=128)))
    t_bt = rdma("sp", lambda e: e.dma_start(
        out=beta_pp[:], allow_slow_non_contiguous=True, in_=beta_d.rearrange("(k p) -> p k", p=128)))
    t_ba = rdma("sp", lambda e: e.dma_start(
        out=ba_pp[:], allow_slow_non_contiguous=True, in_=ba_d.rearrange("(g p) -> p g", p=128)))
    t_fb = rdma("pool", lambda e: e.dma_start(
        out=fb_row[:], in_=fb_d.rearrange("(a d) -> a d", a=1)))
    t_szero = rstep("dve", lambda e: [e.memset(sbf[:], 0.0), e.memset(sbf32[:], 0.0)],
                    writes=[res("sbf"), res("sbf32")])
    for p in range(2):
        rstep("dve", lambda e, p=p: e.memset(Lx[p][:, 0:1], 0.0),
              writes=[res("Lx0")])

    wlist = [("q", wq_d, wqb), ("k", wk_d, wkb), ("a", wa_d, wab), ("v", wv_d, wvb)]
    t_bias_mm = []
    for wi, (wn, wdr, wsb_) in enumerate(wlist):
        for kt in range(8):
            sl = (wi * 8 + kt) % 2
            st = stagv[sl][:, 0:512]
            t_ld = rdma("sp", lambda e, wdr=wdr, kt=kt, st=st: e.dma_start(
                out=st, in_=wdr[kt * 128:(kt + 1) * 128, :]),
                writes=[r_stag[sl]], extra=[t_bt if kt == 0 and wi == 0 else None])
            if wn != "v":
                def mk_mm(kt=kt, st=st, wi=wi):
                    def f(e):
                        out = []
                        for fg in range(4):
                            out.append(e.matmul(
                                PBK[fg][:, wi:wi + 1],
                                st[:, fg * 128:(fg + 1) * 128],
                                beta_pp[:, kt:kt + 1],
                                start=(kt == 0), stop=(kt == 7),
                                skip_group_check=True))
                        return out
                    return f
                t_mm = rstep("pe", mk_mm(), reads=[r_stag[sl]],
                             writes=[rPB[fg] for fg in range(4)] if kt == 0 else (),
                             extra=[t_bt] + ([rPB[fg].w for fg in range(4)]
                                             if kt > 0 else []))
            else:
                t_mm = rstep("pe", lambda e, kt=kt, st=st: e.matmul(
                    pt[0:1, :], beta_pp[:, kt:kt + 1], st,
                    start=(kt == 0), stop=(kt == 7)),
                    reads=[r_stag[sl]],
                    writes=[rpt] if kt == 0 else (),
                    extra=[t_bt] + ([rpt.w] if kt > 0 else []))
            t_bias_mm.append(t_mm)
            t_f = rstep("dve", lambda e, kt=kt, st=st, wsb_=wsb_: e.tensor_scalar(
                wsb_[:, kt * 512:(kt + 1) * 512], st,
                gamma_pp[:, kt:kt + 1], None, OP.mult),
                reads=[r_stag[sl]], writes=[res("wfold", wn, kt)],
                extra=[t_gm, t_mm])
    t_bq = rstep("dve", lambda e: [e.tensor_copy(bq_pp[:, fg:fg + 1],
                                                 PBK[fg][:, 0:1]) for fg in range(4)],
                 reads=[rPB[fg] for fg in range(4)], writes=[res("bq")])
    t_bk = rstep("dve", lambda e: [e.tensor_copy(bk_pp[:, fg:fg + 1],
                                                 PBK[fg][:, 1:2]) for fg in range(4)],
                 reads=[rPB[fg] for fg in range(4)], writes=[res("bk")])
    t_bab = rstep("dve", lambda e: [e.tensor_tensor(
        babias_pp[:, fg:fg + 1], PBK[fg][:, 2:3], ba_pp[:, fg:fg + 1], OP.add)
        for fg in range(4)],
        reads=[rPB[fg] for fg in range(4)], writes=[res("bab")], extra=[t_ba])
    t_bv = rstep("dve", lambda e: e.tensor_copy(bv_row[:], pt[0:1, :]),
                 reads=[rpt], writes=[res("bv")])

    for pg in range(4):
        sl = pg % 2
        st = stagv[sl][:, 0:1024]
        t_ld = rdma("sp", lambda e, pg=pg, st=st: e.dma_start(
            out=st, in_=wo_d[pg * 128:(pg + 1) * 128, :]), writes=[r_stag[sl]])
        rstep("dve", lambda e, pg=pg, st=st: e.tensor_copy(
            wob[:, pg * 1024:(pg + 1) * 1024], st),
            reads=[r_stag[sl]], writes=[res("wob", pg)])

    for rt in range(8):
        p = rt % 2
        st = stagv[p][:, 0:1024]
        t_ld = rdma("sp", lambda e, rt=rt, st=st: e.dma_start(
            out=st, in_=wd_d[rt * 128:(rt + 1) * 128, :]), writes=[r_stag[p]])
        t_bm = rstep("dve", lambda e, st=st, p=p: e.reduce_max(
            bmax[p][:], st.rearrange("q (a b) -> q a b", b=32), AX.X,
            apply_absolute_value=True),
            reads=[r_stag[p]], writes=[res("bmax", p)], small=True)
        t_sc = rstep("dve", lambda e, p=p: e.tensor_scalar(
            sclf[p][:], bmax[p][:], 1.0 / 7.5, 1e-30, OP.mult, OP.max),
            reads=[res("bmax", p)], writes=[res("sclf", p)], small=True)
        t_scb = rstep("dve", lambda e, p=p: e.tensor_copy(sclb[p][:], sclf[p][:]),
                      reads=[res("sclf", p)], writes=[res("sclb", p)], small=True)
        t_rc = rstep("dve", lambda e, p=p: e.reciprocal(sclf[p][:], sclf[p][:]),
                     reads=[res("sclf", p)], writes=[res("sclf", p)], extra=[t_scb],
                     small=True)
        t_rcb = rstep("dve", lambda e, p=p: e.tensor_copy(rcpb[p][:], sclf[p][:]),
                      reads=[res("sclf", p)], writes=[res("rcpb", p)], small=True)
        t_v = rstep("dve", lambda e, st=st, p=p: e.tensor_tensor(
            q_t1[p][:].rearrange("q (a b) -> q a b", b=32),
            st.rearrange("q (a b) -> q a b", b=32),
            rcpb[p].ap().rearrange("q (a o) -> q a o", o=1).broadcast_to((128, 32, 32)),
            OP.mult),
            reads=[r_stag[p], res("rcpb", p)], writes=[res("q_t1")])
        t_av = rstep("act", lambda e, p=p: e.activation(q_av[p][:], q_t1[p][:], AF.Abs),
                     reads=[res("q_t1")], writes=[res("q_av")])
        t_sg = rstep("act", lambda e, p=p: e.activation(q_sg[p][:], q_t1[p][:], AF.Sign),
                     reads=[res("q_t1")], writes=[res("q_sg")])
        t_u = rstep("dve", lambda e, p=p: e._custom_dve(
            OPA, out=q_t1[p][:], in0=q_av[p][:], s0=TH1, s1=TH2, imm2=4.0),
            reads=[res("q_av")], writes=[res("q_t1")])
        t_r2 = rstep("dve", lambda e, p=p: e._custom_dve(
            OPB, out=q_t4[p][:], in0=q_t1[p][:], s0=7.5, s1=8.0, imm2=3.5),
            reads=[res("q_t1")], writes=[res("q_t4")])
        t_n = rstep("dve", lambda e, p=p: e._custom_dve(
            OPC, out=q_t5[p][:], in0=q_t4[p][:], in1=q_t1[p][:], s0=1.5, s1=0.5),
            reads=[res("q_t4"), res("q_t1")], writes=[res("q_t5")])
        t_qa = rstep("dve", lambda e, p=p: e._custom_dve(
            OPD, out=q_t4[p][:], in0=q_av[p][:], in1=q_t5[p][:], s0=TH1, s1=TH2,
            imm2=0.125),
            reads=[res("q_av"), res("q_t5")], writes=[res("q_t4")])
        t_sq = rstep("dve", lambda e, p=p: e.tensor_tensor(
            q_t1[p][:], q_t4[p][:], q_sg[p][:], OP.mult),
            reads=[res("q_t4"), res("q_sg")], writes=[res("q_t1")])
        t_xd = rstep("dve", lambda e, p=p: e.tensor_tensor(
            q_t5[p][:].rearrange("q (a b) -> q a b", b=32),
            q_t1[p][:].rearrange("q (a b) -> q a b", b=32),
            sclb[p].ap().rearrange("q (a o) -> q a o", o=1).broadcast_to((128, 32, 32)),
            OP.mult),
            reads=[res("q_t1"), res("sclb", p)], writes=[res("q_t5")])
        pbank_bf, rp = (pA_bf, rpA) if p == 0 else (pB_bf, rpB)
        t_tr = rstep("pe", lambda e, p=p, pbank_bf=pbank_bf: [
            e.transpose(pbank_bf[:, kt * 128:(kt + 1) * 128],
                        q_t5[p][:, kt * 128:(kt + 1) * 128], c_ident[:])
            for kt in range(8)],
            reads=[res("q_t5")], writes=[rp], extra=[t_id])
        rstep("dve", lambda e, rt=rt, pbank_bf=pbank_bf: [
            e.tensor_copy(wdT[:, kt * 1024 + rt * 128: kt * 1024 + (rt + 1) * 128],
                          pbank_bf[:, kt * 128:(kt + 1) * 128])
            for kt in range(8)],
            reads=[rp], writes=[res("wdT", rt)])

    t_wready = [res("wfold", wn, kt).w for wn, _, _ in wlist for kt in range(8)]
    t_wdTready = [res("wdT", rt).w for rt in range(8)]
    t_wobready = [res("wob", pg).w for pg in range(4)]

    xload = {}
    def load_x(g):
        slot = g % XRING
        xload[g] = rdma("sp", lambda e, g=g, slot=slot: e.dma_start(
            out=x_t[:, slot * D:(slot + 1) * D], in_=x_r[g]), writes=[r_x[slot]])
    for g in range(min(4, NCHUNK)):
        load_x(g)

    for sc in range(nsc if stage >= 2 else 0):
        pp = sc % 2
        for ch in range(NCH_SC):
            g = sc * NCH_SC + ch
            cp = g % 2
            xs = x_t[:, (g % XRING) * D:((g % XRING) + 1) * D]
            xnb = xnbf[:, (g % 4) * D:((g % 4) + 1) * D]
            rx = r_x[g % XRING]
            rxn = r_xnbf[g % 4]
            s = sml[g % 4]
            rs = res("sml", g % 4)
            t_bn = rstep("dve", lambda e, cp=cp, xs=xs: [
                e.bn_stats(bn6[cp][:, 0:6], xs[:, 0:512]),
                e.bn_stats(bn6[cp][:, 6:12], xs[:, 512:1024])],
                reads=[rx], writes=[res("bn6", cp)], extra=[xload[g]], small=True)
            t_ag = rstep("dve", lambda e, cp=cp: [
                e.bn_aggr(mv[cp][:, 0:2], bn6[cp][:, 0:6]),
                e.bn_aggr(mv[cp][:, 2:4], bn6[cp][:, 6:12])],
                reads=[res("bn6", cp)], writes=[res("mv", cp)], small=True)
            rmv = res("mv", cp)
            t_c1 = rstep("dve", lambda e, cp=cp, s=s: [
                e.tensor_tensor(s[:, 11:12], mv[cp][:, 0:1], mv[cp][:, 2:3], OP.add),
                e.tensor_tensor(s[:, 13:14], mv[cp][:, 1:2], mv[cp][:, 3:4], OP.add),
                e.tensor_tensor(s[:, 12:13], mv[cp][:, 0:1], mv[cp][:, 2:3],
                                OP.subtract),
                e.tensor_scalar(s[:, 11:12], s[:, 11:12], 0.5, None, OP.mult),
                e.tensor_scalar(s[:, 13:14], s[:, 13:14], 0.5, None, OP.mult),
                e.tensor_scalar(s[:, 14:15], s[:, 12:13], 0.5, None, OP.mult),
                e.tensor_scalar(s[:, 8:9], s[:, 11:12], -1.0, None, OP.mult),
                e.tensor_tensor(s[:, 15:16], s[:, 14:15], s[:, 14:15], OP.mult)],
                reads=[rmv], writes=[rs], small=True)
            t_c2 = rstep("dve", lambda e, s=s: e.tensor_tensor(
                s[:, 12:13], s[:, 13:14], s[:, 15:16], OP.add),
                reads=[rs], writes=[rs], small=True)
            t_c3 = rstep("dve", lambda e, s=s: e.tensor_scalar(
                s[:, 2:3], s[:, 12:13], EPS, None, OP.add),
                reads=[rs], writes=[rs], small=True)
            t_s1 = rstep("act", lambda e, s=s: e.activation(
                s[:, 0:1], s[:, 12:13], AF.Sqrt, bias=EPS),
                reads=[rs], writes=[rs], extra=[t_cv])
            t_s4 = rstep("dve", lambda e, s=s: e.reciprocal(s[:, 3:4], s[:, 2:3]),
                         reads=[rs], writes=[rs], small=True)
            t_s5 = rstep("dve", lambda e, s=s: e.reciprocal(s[:, 1:2], s[:, 0:1]),
                         reads=[rs], writes=[rs], small=True)
            t_s6 = rstep("dve", lambda e, s=s: e.tensor_tensor(
                s[:, 4:5], s[:, 12:13], s[:, 3:4], OP.mult),
                reads=[rs], writes=[rs], small=True)
            t_s2 = rstep("act", lambda e, s=s: e.activation(
                s[:, 5:6], s[:, 4:5], AF.Sqrt, bias=EPS),
                reads=[rs], writes=[rs], extra=[t_cv])
            t_s7 = rstep("dve", lambda e, s=s: e.reciprocal(s[:, 6:7], s[:, 5:6]),
                         reads=[rs], writes=[rs], small=True)
            t_s8 = rstep("dve", lambda e, s=s: [
                e.tensor_tensor(s[:, 7:8], s[:, 1:2], s[:, 6:7], OP.mult),
                e.tensor_tensor(s[:, 9:10], s[:, 8:9], s[:, 1:2], OP.mult),
                e.tensor_tensor(s[:, 10:11], s[:, 8:9], s[:, 7:8], OP.mult)],
                reads=[rs], writes=[rs], small=True)
            t_xn = rstep("act", lambda e, xs=xs, xnb=xnb, s=s: e.activation(
                xnb, xs, AF.Identity, bias=s[:, 9:10], scale=s[:, 1:2]),
                reads=[rx, rs], writes=[rxn])
            t_x2 = rstep("act", lambda e, xs=xs, s=s, pp=pp, ch=ch: e.activation(
                xn2a[pp][:, ch * D:(ch + 1) * D], xs, AF.Identity,
                bias=s[:, 10:11], scale=s[:, 7:8]),
                reads=[rx, rs], writes=[res("xn2a", pp, ch)])
        if stage < 3 or sc >= sclim:
            continue
        for kt in range(8):
            pbank_bf, rp = (pA_bf, rpA) if kt % 2 == 0 else (pB_bf, rpB)
            t_tr = rstep("pe", lambda e, kt=kt, pbank_bf=pbank_bf, pp=pp: [
                e.transpose(pbank_bf[:, ch * 128:(ch + 1) * 128],
                            xn2a[pp][:, ch * D + kt * 128: ch * D + (kt + 1) * 128],
                            c_ident[:])
                for ch in range(NCH_SC)],
                reads=[res("xn2a", pp, ch) for ch in range(NCH_SC)],
                writes=[rp], extra=[t_id])
            rstep("dve", lambda e, kt=kt, pbank_bf=pbank_bf, pp=pp: e.tensor_copy(
                xn2aF[pp][:, kt * 512:(kt + 1) * 512], pbank_bf[:, 0:512]),
                reads=[rp], writes=[res("xn2aF", pp, kt)],
                extra=(r_stag[pp].wdeps() if sc < 2 else []))
        rxf = [res("xn2aF", pp, kt) for kt in range(8)]

        for fg in range(4 if sub >= 1 else 0):
            t_pz = rstep("pe", lambda e, fg=fg, pp=pp: [
                e.matmul(pa[:], wab[:, kt * 512 + fg * 128: kt * 512 + (fg + 1) * 128],
                         xn2aF[pp][:, kt * 512:(kt + 1) * 512],
                         start=(kt == 0), stop=(kt == 7))
                for kt in range(8)],
                reads=rxf, writes=[rpa], extra=t_wready)
            if sub < 2:
                continue
            t_sig = rstep("act", lambda e, fg=fg, pp=pp: e.activation(
                E1s[pp][:], pa[:], AF.Sigmoid, bias=babias_pp[:, fg:fg + 1]),
                reads=[rpa, res("bab")], writes=[res("E1s")], extra=[t_bab])
            t_la = rstep("act", lambda e, pp=pp: e.activation(
                la_s[pp][:], E1s[pp][:], AF.Ln),
                reads=[res("E1s")], writes=[res("la")])
            t_scan = rstep("dve", lambda e, pp=pp: e.tensor_tensor_scan(
                Lx[pp][:, 1:513], la_s[pp][:], la_s[pp][:], 0.0, OP.add, OP.bypass),
                reads=[res("la"), res("Lx0")], writes=[res("Lx")])
            if sub < 3:
                continue
            t_pq = rstep("pe", lambda e, fg=fg, pp=pp: [
                e.matmul(pq[:], wqb[:, kt * 512 + fg * 128: kt * 512 + (fg + 1) * 128],
                         xn2aF[pp][:, kt * 512:(kt + 1) * 512],
                         start=(kt == 0), stop=(kt == 7))
                for kt in range(8)],
                reads=rxf, writes=[rpq], extra=t_wready)
            t_pk = rstep("pe", lambda e, fg=fg, pp=pp: [
                e.matmul(pk[:], wkb[:, kt * 512 + fg * 128: kt * 512 + (fg + 1) * 128],
                         xn2aF[pp][:, kt * 512:(kt + 1) * 512],
                         start=(kt == 0), stop=(kt == 7))
                for kt in range(8)],
                reads=rxf, writes=[rpk], extra=t_wready)
            t_kraw = rstep("dve", lambda e, fg=fg, pp=pp: e.tensor_scalar(
                kraw[pp][:], pk[:], bk_pp[:, fg:fg + 1], None, OP.add),
                reads=[rpk, res("bk")], writes=[res("kraw", pp)], extra=[t_bk])
            t_ksq = rstep("act", lambda e, pp=pp: e.activation(
                ksqF[pp][:], kraw[pp][:], AF.Square),
                reads=[res("kraw", pp)], writes=[res("ksqF", pp)])
            for ch in range(NCH_SC if sub >= 4 else 0):
                cs = ch * 128
                t_nl = rstep("dve", lambda e, pp=pp, ch=ch, cs=cs: e.tensor_scalar(
                    nlc[pp][:, ch:ch + 1], Lx[pp][:, cs:cs + 1], -1.0, None, OP.mult),
                    reads=[res("Lx")], writes=[res("nlc", pp)], small=True)
                t_e1 = rstep("act", lambda e, pp=pp, ch=ch, cs=cs: e.activation(
                    E1s[pp][:, cs:cs + 128], Lx[pp][:, cs + 1:cs + 129], AF.Exp,
                    bias=nlc[pp][:, ch:ch + 1]),
                    reads=[res("Lx"), res("nlc", pp)], writes=[res("E1s")])
                t_m1 = rstep("dve", lambda e, pp=pp, cs=cs: e.tensor_scalar(
                    ekarg[pp][:, cs:cs + 128], Lx[pp][:, cs + 1:cs + 129], -1.0,
                    Lx[pp][:, cs:cs + 1], OP.mult, OP.add),
                    reads=[res("Lx")], writes=[res("ekarg")])
                t_m2 = rstep("dve", lambda e, pp=pp, cs=cs: e.tensor_scalar(
                    ekarg[pp][:, cs:cs + 128], ekarg[pp][:, cs:cs + 128], 80.0, None,
                    OP.min),
                    reads=[res("ekarg")], writes=[res("ekarg")])
                t_ek = rstep("act", lambda e, pp=pp, cs=cs: e.activation(
                    e1k[pp][:, cs:cs + 128], ekarg[pp][:, cs:cs + 128], AF.Exp),
                    reads=[res("ekarg")], writes=[res("e1k", pp)])
                rstep("dve", lambda e, pp=pp, fg=fg, cs=cs: e.scalar_tensor_tensor(
                    qtF[pp][:, fg * 512 + cs: fg * 512 + cs + 128],
                    pq[:, cs:cs + 128], bq_pp[:, fg:fg + 1], E1s[pp][:, cs:cs + 128],
                    OP.add, OP.mult),
                    reads=[rpq, res("E1s"), res("bq")],
                    writes=[res("qtF", pp, fg, ch)], extra=[t_bq])
                rstep("dve", lambda e, pp=pp, fg=fg, cs=cs: e.tensor_tensor(
                    ktF[pp][:, fg * 512 + cs: fg * 512 + cs + 128],
                    kraw[pp][:, cs:cs + 128], e1k[pp][:, cs:cs + 128], OP.mult),
                    reads=[res("kraw", pp), res("e1k", pp)],
                    writes=[res("ktF", pp, fg, ch)])
                if sub < 5:
                    continue
                kb_bf, rkb = (pA_bf, rpA) if fg % 2 == 0 else (pB_bf, rpB)
                qb_bf, rqb = (pB_bf, rpB) if fg % 2 == 0 else (pA_bf, rpA)
                t_tr = rstep("pe", lambda e, pp=pp, fg=fg, cs=cs, kb_bf=kb_bf,
                             qb_bf=qb_bf: [
                    e.transpose(kb_bf[:, 0:128],
                                ktF[pp][:, fg * 512 + cs: fg * 512 + cs + 128],
                                c_ident[:]),
                    e.transpose(qb_bf[:, 0:128],
                                ksqF[pp][:, cs:cs + 128], c_ident[:])],
                    reads=[res("ktF", pp, fg, ch), res("ksqF", pp)],
                    writes=[rkb, rqb], extra=[t_id])
                rstep("act", lambda e, pp=pp, fg=fg, ch=ch, kb_bf=kb_bf:
                      e.activation(
                          ktok[pp][:, ch * 512 + fg * 128: ch * 512 + (fg + 1) * 128],
                          kb_bf[:, 0:128], AF.Copy),
                      reads=[rkb], writes=[res("ktok", pp, fg, ch)])
                rstep("dve", lambda e, pp=pp, fg=fg, ch=ch, qb_bf=qb_bf:
                      e.tensor_reduce(
                          rkt[pp][:, ch * 8 + fg * 2: ch * 8 + (fg + 1) * 2],
                          qb_bf[:, 0:128].rearrange("q (a b) -> q a b", b=64),
                          AX.X, OP.add),
                      reads=[rqb], writes=[res("rkt", pp, fg, ch)], small=True)
            if sub < 6:
                continue
            rstep("dve", lambda e, pp=pp, fg=fg: e.tensor_copy(
                ECt[pp][:, fg * 4:(fg + 1) * 4],
                AP(E1s[pp].ap().tensor, E1s[pp].ap().offset + 127, [[512, 128], [128, 4]])),
                reads=[res("E1s")], writes=[res("ECt", pp, fg)], small=True)

        if stage < 4:
            continue
        for ch in range(min(NCH_SC, chlim)):
            g = sc * NCH_SC + ch
            cp = g % 2
            cs = ch * 128
            xs = x_t[:, (g % XRING) * D:((g % XRING) + 1) * D]
            xnb = xnbf[:, (g % 4) * D:((g % 4) + 1) * D]
            rx = r_x[g % XRING]
            rxn = r_xnbf[g % 4]
            s4 = sml[g % 4]
            rs4 = res("sml", g % 4)
            t_xf = rstep("act", lambda e, cp=cp, xs=xs, s4=s4: e.activation(
                outb[cp][:], xs, AF.Identity, bias=s4[:, 9:10], scale=s4[:, 1:2]),
                reads=[rx, rs4], writes=[res("outb", cp)])
            t_pv = rstep("pe", lambda e, pp=pp, cs=cs: [
                e.matmul(pv[:],
                         xn2aF[pp][:, kt * 512 + cs: kt * 512 + cs + 128],
                         wvb[:, kt * 512:(kt + 1) * 512],
                         start=(kt == 0), stop=False)
                for kt in range(8)] + [
                e.matmul(pv[:], c_ones[:], bv_row[:], start=False, stop=True)],
                reads=rxf + [res("bv")], writes=[rpv],
                extra=t_wready + [t_on, t_bv])
            t_vt = rstep("act", lambda e, cp=cp: e.activation(
                vtok[cp][:], pv[:], AF.Copy),
                reads=[rpv], writes=[res("vtok", cp)])
            t_rk1 = rstep("act", lambda e, pp=pp, cp=cp, ch=ch: e.activation(
                rkf[cp][:], rkt[pp][:, ch * 8: ch * 8 + 8], AF.Sqrt),
                reads=[res("rkt", pp, fgx, ch) for fgx in range(4)],
                writes=[res("rkf", cp)])
            t_rk2 = rstep("dve", lambda e, cp=cp: e.tensor_scalar(
                rkf[cp][:], rkf[cp][:], 1e-12, None, OP.max),
                reads=[res("rkf", cp)], writes=[res("rkf", cp)], small=True)
            t_rk3 = rstep("dve", lambda e, cp=cp: e.reciprocal(rkf[cp][:], rkf[cp][:]),
                          reads=[res("rkf", cp)], writes=[res("rkf", cp)],
                          extra=[t_rk2], small=True)
            t_vp = rstep("dve", lambda e, cp=cp: [
                e.tensor_scalar(vpr[cp][:, h * 64:(h + 1) * 64],
                                vtok[cp][:, h * 64:(h + 1) * 64],
                                rkf[cp][:, h:h + 1], None, OP.mult)
                for h in range(8)],
                reads=[res("vtok", cp), res("rkf", cp)], writes=[res("vpr", cp)])
            if g < WARM:
                rstep("dve", lambda e, cp=cp: e.tensor_scalar(
                    vpr[cp][:], vpr[cp][:], wmkt[:, 0:1], None, OP.mult),
                    reads=[res("vpr", cp), res("wmk")], writes=[res("vpr", cp)],
                    extra=[t_wmk])
            t_bm = rstep("dve", lambda e, cp=cp: e.reduce_max(
                bmax[cp][:], outb[cp][:].rearrange("q (a b) -> q a b", b=32), AX.X,
                apply_absolute_value=True),
                reads=[res("outb", cp)], writes=[res("bmax", cp)], small=True)
            t_sc_ = rstep("dve", lambda e, cp=cp: e.tensor_scalar(
                sclf[cp][:], bmax[cp][:], 1.0 / 7.5, 1e-30, OP.mult, OP.max),
                reads=[res("bmax", cp)], writes=[res("sclf", cp)], small=True)
            t_scb = rstep("dve", lambda e, cp=cp: e.tensor_copy(sclb[cp][:], sclf[cp][:]),
                          reads=[res("sclf", cp)], writes=[res("sclb", cp)], small=True)
            t_rc = rstep("dve", lambda e, cp=cp: e.reciprocal(sclf[cp][:], sclf[cp][:]),
                         reads=[res("sclf", cp)], writes=[res("sclf", cp)],
                         extra=[t_scb], small=True)
            t_rcb = rstep("dve", lambda e, cp=cp: e.tensor_copy(rcpb[cp][:], sclf[cp][:]),
                          reads=[res("sclf", cp)], writes=[res("rcpb", cp)], small=True)
            t_v = rstep("dve", lambda e, cp=cp: e.tensor_tensor(
                q_t1[cp][:].rearrange("q (a b) -> q a b", b=32),
                outb[cp][:].rearrange("q (a b) -> q a b", b=32),
                rcpb[cp].ap().rearrange("q (a o) -> q a o", o=1).broadcast_to((128, 32, 32)),
                OP.mult),
                reads=[res("outb", cp), res("rcpb", cp)], writes=[res("q_t1")])
            t_av = rstep("act", lambda e, cp=cp: e.activation(
                q_av[cp][:], q_t1[cp][:], AF.Abs),
                reads=[res("q_t1")], writes=[res("q_av")])
            t_sg = rstep("act", lambda e, cp=cp: e.activation(
                q_sg[cp][:], q_t1[cp][:], AF.Sign),
                reads=[res("q_t1")], writes=[res("q_sg")])
            t_u = rstep("dve", lambda e, cp=cp: e._custom_dve(
                OPA, out=q_t1[cp][:], in0=q_av[cp][:], s0=TH1, s1=TH2, imm2=4.0),
                reads=[res("q_av")], writes=[res("q_t1")])
            t_r2q = rstep("dve", lambda e, cp=cp: e._custom_dve(
                OPB, out=q_t4[cp][:], in0=q_t1[cp][:], s0=7.5, s1=8.0, imm2=3.5),
                reads=[res("q_t1")], writes=[res("q_t4")])
            t_n = rstep("dve", lambda e, cp=cp: e._custom_dve(
                OPC, out=q_t5[cp][:], in0=q_t4[cp][:], in1=q_t1[cp][:], s0=1.5, s1=0.5),
                reads=[res("q_t4"), res("q_t1")], writes=[res("q_t5")])
            t_qa = rstep("dve", lambda e, cp=cp: e._custom_dve(
                OPD, out=q_t4[cp][:], in0=q_av[cp][:], in1=q_t5[cp][:], s0=TH1, s1=TH2,
                imm2=0.125),
                reads=[res("q_av"), res("q_t5")], writes=[res("q_t4")])
            t_sq = rstep("dve", lambda e, cp=cp: e.tensor_tensor(
                q_t1[cp][:], q_t4[cp][:], q_sg[cp][:], OP.mult),
                reads=[res("q_t4"), res("q_sg")], writes=[res("q_t1")])
            t_xd = rstep("dve", lambda e, cp=cp: e.tensor_tensor(
                xdc[cp][:].rearrange("q (a b) -> q a b", b=32),
                q_t1[cp][:].rearrange("q (a b) -> q a b", b=32),
                sclb[cp].ap().rearrange("q (a o) -> q a o", o=1).broadcast_to((128, 32, 32)),
                OP.mult),
                reads=[res("q_t1"), res("sclb", cp)], writes=[res("xdc")])

            t_Ae = rstep("pe", lambda e, pp=pp, cs=cs: [
                e.matmul(pA[:, fgx * 128:(fgx + 1) * 128],
                         ktF[pp][0:64, fgx * 512 + cs: fgx * 512 + cs + 128],
                         qtF[pp][0:64, fgx * 512 + cs: fgx * 512 + cs + 128],
                         start=True, stop=True, skip_group_check=True)
                for fgx in range(4)],
                reads=[res("ktF", pp, fgx, ch) for fgx in range(4)] +
                      [res("qtF", pp, fgx, ch) for fgx in range(4)],
                writes=[rpA])
            t_Ao = rstep("pe", lambda e, pp=pp, cs=cs: [
                e.matmul(pB_[:, fgx * 128:(fgx + 1) * 128],
                         ktF[pp][64:128, fgx * 512 + cs: fgx * 512 + cs + 128],
                         qtF[pp][64:128, fgx * 512 + cs: fgx * 512 + cs + 128],
                         start=True, stop=True, skip_group_check=True)
                for fgx in range(4)],
                reads=[res("ktF", pp, fgx, ch) for fgx in range(4)] +
                      [res("qtF", pp, fgx, ch) for fgx in range(4)],
                writes=[rpB])
            t_Aev = rstep("dve", lambda e, cp=cp: e.tensor_tensor(
                Aev[cp][:], pA[:], c_tri4[:], OP.mult),
                reads=[rpA], writes=[res("Aev")], extra=[t_tri])
            t_Aod = rstep("dve", lambda e, cp=cp: e.tensor_tensor(
                Aod[cp][:], pB_[:], c_tri4[:], OP.mult),
                reads=[rpB], writes=[res("Aod")], extra=[t_tri])
            def mk_o(pp=pp, cp=cp, cs=cs):
                def f(e):
                    out = []
                    for pg in range(4):
                        out.append(e.matmul(
                            po[:, pg * 128:(pg + 1) * 128],
                            sbf[:, pg * 128:(pg + 1) * 128],
                            qtF[pp][:, pg * 512 + cs: pg * 512 + cs + 128],
                            start=True, stop=False, skip_group_check=True))
                        out.append(e.matmul(
                            po[0:64, pg * 128:(pg + 1) * 128],
                            vpr[cp][:, pg * 128: pg * 128 + 64],
                            Aev[cp][:, pg * 128:(pg + 1) * 128],
                            start=False, stop=False, skip_group_check=True))
                        out.append(e.matmul(
                            po[64:128, pg * 128:(pg + 1) * 128],
                            vpr[cp][:, pg * 128 + 64: pg * 128 + 128],
                            Aod[cp][:, pg * 128:(pg + 1) * 128],
                            start=False, stop=True, skip_group_check=True))
                    return out
                return f
            t_o = rstep("pe", mk_o(),
                        reads=[res("sbf"), res("vpr", cp), res("Aev"),
                               res("Aod")] +
                              [res("qtF", pp, fgx, ch) for fgx in range(4)],
                        writes=[rpo])
            t_T0 = rstep("pe", lambda e, pp=pp, cp=cp, ch=ch: [
                e.matmul(pt[:, pg * 128:(pg + 1) * 128],
                         ktok[pp][:, ch * 512 + pg * 128: ch * 512 + (pg + 1) * 128],
                         vpr[cp][:, pg * 128:(pg + 1) * 128],
                         start=True, stop=True, skip_group_check=True)
                for pg in range(4)],
                reads=[res("ktok", pp, fgx, ch) for fgx in range(4)] +
                      [res("vpr", cp)],
                writes=[rpt])
            t_su = rstep("dve", lambda e: e.tensor_tensor(
                sbf32[:], sbf32[:], pt[:], OP.add),
                reads=[res("sbf32"), rpt], writes=[res("sbf32")], extra=[t_o])
            t_s2_ = rstep("dve", lambda e, cp=cp, pp=pp, ch=ch: [
                e.scalar_tensor_tensor(
                    sbf32[:, pg * 128:(pg + 1) * 128],
                    sbf32[:, pg * 128:(pg + 1) * 128],
                    ECt[pp][:, pg * 4 + ch: pg * 4 + ch + 1],
                    c_bd[:], OP.mult, OP.mult)
                for pg in range(4)],
                reads=[res("sbf32")] + [res("ECt", pp, fgx) for fgx in range(4)],
                writes=[res("sbf32")], extra=[t_bd])
            t_scast = rstep("act", lambda e: e.activation(
                sbf[:], sbf32[:], AF.Copy),
                reads=[res("sbf32")], writes=[res("sbf")])
            t_oe = rstep("act", lambda e, cp=cp: e.activation(
                osb[cp][:], po[:], AF.Copy),
                reads=[rpo], writes=[res("osb", cp)])
            for half in range(2):
                pbank_bf, rp = (pA_bf, rpA) if half == 0 else (pB_bf, rpB)
                t_tr = rstep("pe", lambda e, cp=cp, half=half, pbank_bf=pbank_bf: [
                    e.transpose(pbank_bf[:, kk * 128:(kk + 1) * 128],
                                xdc[cp][:, (half * 4 + kk) * 128:
                                         (half * 4 + kk + 1) * 128],
                                c_ident[:])
                    for kk in range(4)],
                    reads=[res("xdc")], writes=[rp], extra=[t_id])
                rstep("dve", lambda e, cp=cp, half=half, pbank_bf=pbank_bf:
                      e.tensor_copy(xdF[cp][:, half * 512:(half + 1) * 512],
                                    pbank_bf[:, 0:512]),
                      reads=[rp], writes=[res("xdF", cp, half)])
            def mk_wo(mhalf, bank, cp=cp, xnb=xnb):
                def f(e):
                    out = [e.matmul(bank[:], c_ident[:],
                                    xnb[:, mhalf * 512:(mhalf + 1) * 512],
                                    start=True, stop=False)]
                    for pg in range(4):
                        out.append(e.matmul(
                            bank[:],
                            osb[cp][:, pg * 128:(pg + 1) * 128],
                            wob[:, pg * 1024 + mhalf * 512:
                                pg * 1024 + (mhalf + 1) * 512],
                            start=False, stop=(pg == 3)))
                    return out
                return f
            if stage < 5:
                continue
            t_wo0 = rstep("pe", mk_wo(0, pq), reads=[res("osb", cp), rxn],
                          writes=[rpq], extra=t_wobready + [t_id])
            t_wo1 = rstep("pe", mk_wo(1, pk), reads=[res("osb", cp), rxn],
                          writes=[rpk], extra=t_wobready + [t_id])
            def mk_ffn(mhalf, bank, cp=cp):
                def f(e):
                    out = []
                    for kt in range(8):
                        out.append(e.matmul(
                            bank[:],
                            xdF[cp][:, kt * 128:(kt + 1) * 128],
                            wdT[:, kt * 1024 + mhalf * 512:
                                kt * 1024 + (mhalf + 1) * 512],
                            start=(kt == 0), stop=False))
                    out.append(e.matmul(bank[:], c_ones[:],
                                        fb_row[:, mhalf * 512:(mhalf + 1) * 512],
                                        start=False, stop=True))
                    return out
                return f
            t_ff0 = rstep("pe", mk_ffn(0, pa),
                          reads=[res("xdF", cp, 0), res("xdF", cp, 1)],
                          writes=[rpa], extra=t_wdTready + [t_on, t_fb])
            t_ff1 = rstep("pe", mk_ffn(1, pv),
                          reads=[res("xdF", cp, 0), res("xdF", cp, 1)],
                          writes=[rpv], extra=t_wdTready + [t_on, t_fb])
            t_g0 = rstep("act", lambda e, cp=cp: e.activation(
                gsb[cp][:, 0:512], pa[:], AF.Silu), reads=[rpa],
                writes=[res("gsb")])
            t_g1 = rstep("act", lambda e, cp=cp: e.activation(
                gsb[cp][:, 512:1024], pv[:], AF.Silu), reads=[rpv],
                writes=[res("gsb")])
            t_w0 = rstep("dve", lambda e, cp=cp: e.tensor_tensor(
                wsb[cp][:, 0:512], pq[:], gsb[cp][:, 0:512], OP.mult),
                reads=[rpq, res("gsb")], writes=[res("wsb", cp)],
                extra=[t_g0])
            t_w1 = rstep("dve", lambda e, cp=cp: e.tensor_tensor(
                wsb[cp][:, 512:1024], pk[:], gsb[cp][:, 512:1024], OP.mult),
                reads=[rpk, res("gsb")], writes=[res("wsb", cp)],
                extra=[t_g1])
            if g >= WARM:
                t_out = rstep("pool", lambda e, cp=cp, xs=xs: e.tensor_tensor(
                    outb[cp][:], wsb[cp][:], xs, OP.add),
                    reads=[res("wsb", cp), rx], writes=[res("outb", cp)],
                    extra=[t_w0, t_w1])
                rdma("sp", lambda e, cp=cp, g=g: e.dma_start(
                    out=out_r[g - WARM], in_=outb[cp][:]),
                    reads=[res("outb", cp)])
            if g + 4 < NCHUNK:
                load_x(g + 4)

    ltab = dict(locals())
    for dn in dumps:
        t = ltab[dn]
        if isinstance(t, list):
            t = t[0]
        shape = list(t.ap().shape)
        dt_ = t.ap().dtype
        dd = nc.dram_tensor("dump_" + dn, shape, dt_, kind="ExternalOutput").ap()
        rdma("sp", lambda e, dd=dd, t=t: e.dma_start(out=dd, in_=t[:]),
             extra=[("s", "dve", S.cnt["dve"]), ("s", "act", S.cnt["act"]),
                    ("s", "pe", S.cnt["pe"]), ("s", "pool", S.cnt["pool"])])

    total_sp = S.dcnt["sp"]
    S.step("pool", lambda e: e.engine_nop(), deps=[("d", "sp", total_sp)])

    with (
        nc.semaphore("s_sp") as s_sp, nc.semaphore("s_act") as s_act,
        nc.semaphore("s_pe") as s_pe, nc.semaphore("s_dve") as s_dve,
        nc.semaphore("s_pool") as s_pool,
        nc.semaphore("d_sp") as d_sp, nc.semaphore("d_act") as d_act,
        nc.semaphore("d_pool") as d_pool,
        nc.Block() as block,
    ):
        sems = {"sp": s_sp, "act": s_act, "pe": s_pe, "dve": s_dve, "pool": s_pool}
        dsems = {"sp": d_sp, "act": d_act, "pool": d_pool}
        S.emit(block, sems, dsems)

    mybir.codegen_inst_isa_subclasses(nc)
    return nc


_NC_CACHE = {}

def kernel(L0_attn_x, L0_attn_Wq, L0_attn_Wk, L0_attn_Wv, L0_attn_Wa, L0_attn_ba,
           L0_attn_Wo, L0_attn_gamma, L0_attn_beta, L0_ffn_weight, L0_ffn_bias):
    if "nc" not in _NC_CACHE:
        _NC_CACHE["nc"] = build_program()
    nc = _NC_CACHE["nc"]

    x = np.ascontiguousarray(np.asarray(L0_attn_x, dtype=np.float32))
    B, T, Dm = x.shape
    nseg = 4
    xs = x.reshape(B, nseg, SEG, Dm)

    ident = np.eye(128, dtype=ml_dtypes.bfloat16)
    ones = np.ones((1, 128), dtype=ml_dtypes.bfloat16)
    tri = np.triu(np.ones((128, 128), dtype=np.float32)).astype(ml_dtypes.bfloat16)
    tri4 = np.tile(tri, (1, 4))
    bd = np.zeros((128, 128), dtype=np.float32)
    bd[:64, :64] = 1.0
    bd[64:, 64:] = 1.0
    bd = bd.astype(ml_dtypes.bfloat16)
    cvals = np.zeros((128, 4), dtype=np.float32)
    cvals[:, 0] = EPS

    common = dict(
        wq=np.ascontiguousarray(np.asarray(L0_attn_Wq, np.float32)),
        wk=np.ascontiguousarray(np.asarray(L0_attn_Wk, np.float32)),
        wv=np.ascontiguousarray(np.asarray(L0_attn_Wv, np.float32)),
        wa=np.ascontiguousarray(np.asarray(L0_attn_Wa, np.float32)),
        ba=np.ascontiguousarray(np.asarray(L0_attn_ba, np.float32)),
        wo=np.ascontiguousarray(np.asarray(L0_attn_Wo, np.float32)),
        gamma=np.ascontiguousarray(np.asarray(L0_attn_gamma, np.float32)),
        beta=np.ascontiguousarray(np.asarray(L0_attn_beta, np.float32)),
        wd=np.ascontiguousarray(np.asarray(L0_ffn_weight, np.float32)),
        fbias=np.ascontiguousarray(np.asarray(L0_ffn_bias, np.float32)),
        c_ident=ident, c_ones=ones, c_tri4=tri4, c_bd=bd, c_vals=cvals,
    )
    halo = WARM * CH
    in_maps = []
    for c in range(8):
        b, s = c // nseg, c % nseg
        m = dict(common)
        prev = xs[b, s - 1][-halo:] if s > 0 else xs[b, 0][:halo]
        m["x_in"] = np.ascontiguousarray(np.concatenate([prev, xs[b, s]], axis=0))
        m["wmk"] = np.full((128, 1), 0.0 if s == 0 else 1.0, np.float32)
        in_maps.append(m)

    res = run_bass_kernel_spmd(nc, in_maps, core_ids=list(range(8)))
    out = np.empty((B, nseg, SEG, Dm), dtype=np.float32)
    for c in range(8):
        b, s = c // nseg, c % nseg
        out[b, s] = np.asarray(res.results[c]["out"], dtype=np.float32)
    return out.reshape(B, T, Dm)


# version 06
# speedup vs baseline: 1.0026x; 1.0026x over previous
import numpy as np
import ml_dtypes
import concourse.bass as bass
import concourse.mybir as mybir
from concourse.bass_types import AP
from concourse.bass_utils import run_bass_kernel_spmd
from concourse import dve_ops
from concourse.dve_op import DveOp
from concourse.dve_spec import Spec, Src0, Src1, C0, C1, C2, Zero, One, lower as dve_lower
from concourse.dve_table_gen import dve_ver_for

F32 = mybir.dt.float32
BF16 = mybir.dt.bfloat16
AF = mybir.ActivationFunctionType
OP = mybir.AluOpType
AX = mybir.AxisListType

_VER = dve_ver_for("TRN2")

def _f64(x):
    return np.float64 if x.dtype == np.float64 else np.float32

def _mk_op(name, spec, reference, row):
    for o in dve_ops.OPS:
        if o.name == name:
            return o
    sha = dve_lower(spec, ver=_VER, perf_en=False, name=name).sha(_VER)
    op = DveOp(name, spec, subdim=False, uops_sha={_VER: sha}, reference=reference)
    dve_ops.OPS.append(op)
    dve_ops._SUB_OPCODE_FOR_NAME[name] = row
    dve_ops.CUSTOM_DVE_SPECS[name] = spec
    dve_ops.CUSTOM_DVE_REFS[name] = reference
    return op

_t1 = Src0 >= C0
_t2 = Src0 >= C1
_bodyA = (Src0 + Src0) * ((C2 - (_t1 + _t1)) - _t2)
def _refA(in0, in1, s0, s1, imm2):
    f = _f64(in0); a = in0.astype(f)
    return ((a + a) * (imm2 - 2.0 * (a >= s0) - 1.0 * (a >= s1))).astype(f)
_c2n = One + One
_c4n = _c2n + _c2n
_r1n = Src0 - (Src0 >= C0) * C1
_bodyB = _r1n - (_r1n >= C2) * _c4n
def _refB(in0, in1, s0, s1, imm2):
    f = _f64(in0); u = in0.astype(f)
    r1 = u - (u >= s0) * s1
    return (r1 - (r1 >= imm2) * 4.0).astype(f)
_r3n = Src0 - (Src0 >= C0) * _c2n
_r4n = _r3n - (_r3n >= C1)
_bodyC = Src1 - _r4n
def _refC(in0, in1, s0, s1, imm2):
    f = _f64(in0); r2 = in0.astype(f); u = in1.astype(f)
    r3 = r2 - 2.0 * (r2 >= s0)
    r4 = r3 - 1.0 * (r3 >= s1)
    return (u - r4).astype(f)
_bodyD = Src1 * (C2 * ((One + (Src0 >= C0)) * (One + (Src0 >= C1))))
def _refD(in0, in1, s0, s1, imm2):
    f = _f64(in0); av = in0.astype(f); n = in1.astype(f)
    return (n * (imm2 * (1.0 + (av >= s0)) * (1.0 + (av >= s1)))).astype(f)

OPA = _mk_op("ATH_QA", Spec(body=_bodyA), _refA, 1)
OPB = _mk_op("ATH_QB", Spec(body=_bodyB), _refB, 2)
OPC = _mk_op("ATH_QC", Spec(body=_bodyC), _refC, 3)
OPD = _mk_op("ATH_QD", Spec(body=_bodyD), _refD, 4)

TH1, TH2 = 1.9375, 3.875

ENGS = ("sp", "act", "pe", "dve", "pool")
_BLOCK_ATTR = {"sp": "sync", "act": "scalar", "pe": "tensor",
               "dve": "vector", "pool": "gpsimd"}

class Sched:
    def __init__(self):
        self.plan = {e: [] for e in ENGS}
        self.cnt = {e: 0 for e in ENGS}
        self.dcnt = {e: 0 for e in ENGS}

    def _waits(self, eng, deps, kind, self_wait=0):
        waits = {}
        for tok in deps:
            if tok is None:
                continue
            if isinstance(tok, (list, tuple)) and tok and isinstance(tok[0], (list, tuple)):
                raise TypeError("nested dep list")
            k, e, c = tok
            if k == "s" and e == eng:
                continue
            key = (k, e)
            if waits.get(key, 0) < c:
                waits[key] = c
        if self_wait:
            waits[("s", eng)] = self_wait
        return waits

    def step(self, eng, fn, deps=(), self_wait=0):
        waits = self._waits(eng, deps, "s", self_wait=self_wait)
        self.plan[eng].append((waits, fn, "s"))
        self.cnt[eng] += 1
        return ("s", eng, self.cnt[eng])

    def dma(self, eng, fn, deps=()):
        waits = self._waits(eng, deps, "d")
        self.plan[eng].append((waits, fn, "d"))
        self.dcnt[eng] += 1
        return ("d", eng, self.dcnt[eng])

    def emit(self, block, sems, dsems):
        for ename in ENGS:
            items = self.plan[ename]
            def make_body(items=items, ename=ename):
                def body(eng):
                    marks = {}
                    for waits, fn, kind in items:
                        for (k, e), c in sorted(waits.items()):
                            if marks.get((k, e), 0) >= c:
                                continue
                            marks[(k, e)] = c
                            if k == "s":
                                eng.wait_ge(sems[e], c)
                            else:
                                eng.wait_ge(dsems[e], 16 * c)
                        last = fn(eng)
                        if isinstance(last, list):
                            last = last[-1]
                        if kind == "s":
                            last.then_inc(sems[ename], 1)
                        else:
                            last.then_inc(dsems[ename], 16)
                return body
            getattr(block, _BLOCK_ATTR[ename])(make_body())


class Res:
    def __init__(self):
        self.w = None
        self.readers = []
    def wdeps(self):
        return ([self.w] if self.w else []) + self.readers
    def rdeps(self):
        return [self.w] if self.w else []
    def wrote(self, t):
        self.w = t
        self.readers = []
    def read(self, t):
        self.readers.append(t)


SEG, D, CH, NCH_SC = 4096, 1024, 128, 4
SC = CH * NCH_SC
NSC = SEG // SC
WARM = 4
SEGX = SEG + WARM * CH
NCHUNK = SEGX // CH
NSCT = NCHUNK // NCH_SC
XRING = 4
EPS = 1e-5


STAGE = 5
SUB = 9
SCLIM = 99

def build_program(stage=None, sub=None, sclim=None, nsc=None, dumps=(), chlim=4):
    stage = STAGE if stage is None else stage
    sub = SUB if sub is None else sub
    sclim = SCLIM if sclim is None else sclim
    nsc = NSCT if nsc is None else nsc
    nc = bass.Bass("TRN2", target_bir_lowering=False)

    x_in = nc.dram_tensor("x_in", [SEGX, D], F32, kind="ExternalInput").ap()
    wq_d = nc.dram_tensor("wq", [D, 512], F32, kind="ExternalInput").ap()
    wk_d = nc.dram_tensor("wk", [D, 512], F32, kind="ExternalInput").ap()
    wv_d = nc.dram_tensor("wv", [D, 512], F32, kind="ExternalInput").ap()
    wa_d = nc.dram_tensor("wa", [D, 512], F32, kind="ExternalInput").ap()
    ba_d = nc.dram_tensor("ba", [512], F32, kind="ExternalInput").ap()
    wo_d = nc.dram_tensor("wo", [512, D], F32, kind="ExternalInput").ap()
    gamma_d = nc.dram_tensor("gamma", [D], F32, kind="ExternalInput").ap()
    beta_d = nc.dram_tensor("beta", [D], F32, kind="ExternalInput").ap()
    wd_d = nc.dram_tensor("wd", [D, D], F32, kind="ExternalInput").ap()
    fb_d = nc.dram_tensor("fbias", [D], F32, kind="ExternalInput").ap()
    ident_d = nc.dram_tensor("c_ident", [128, 128], BF16, kind="ExternalInput").ap()
    ones_d = nc.dram_tensor("c_ones", [1, 128], BF16, kind="ExternalInput").ap()
    tri4_d = nc.dram_tensor("c_tri4", [128, 512], BF16, kind="ExternalInput").ap()
    bd_d = nc.dram_tensor("c_bd", [128, 128], BF16, kind="ExternalInput").ap()
    cv_d = nc.dram_tensor("c_vals", [128, 4], F32, kind="ExternalInput").ap()
    wmk_d = nc.dram_tensor("wmk", [128, 1], F32, kind="ExternalInput").ap()
    out_d = nc.dram_tensor("out", [SEG, D], F32, kind="ExternalOutput").ap()

    x_r = x_in.rearrange("(n p) d -> n p d", p=128)
    out_r = out_d.rearrange("(n p) d -> n p d", p=128)

    A = lambda name, shape, dt=BF16: nc.alloc_sbuf_tensor(name, shape, dt)

    c_ident = A("c_identt", [128, 128]); c_ones = A("c_onest", [1, 128])
    c_tri4 = A("c_tri4t", [128, 512]); c_bd = A("c_bdt", [128, 128])
    c_vals = A("c_valst", [128, 4], F32)
    wmkt = A("wmkt", [128, 1], F32)
    gamma_pp = A("gamma_pp", [128, 8], F32)
    beta_pp = A("beta_pp", [128, 8], F32)
    ba_pp = A("ba_pp", [128, 4], F32)
    wqb = A("wqb", [128, 4096]); wkb = A("wkb", [128, 4096])
    wab = A("wab", [128, 4096]); wvb = A("wvb", [128, 4096])
    wdT = A("wdT", [128, 8192])
    wob = A("wob", [128, 4096])
    bq_pp = A("bq_pp", [128, 4], F32); bk_pp = A("bk_pp", [128, 4], F32)
    babias_pp = A("babias_pp", [128, 4], F32)
    bv_row = A("bv_row", [1, 512]); fb_row = A("fb_row", [1, 1024])
    sbf = A("sbf", [128, 512])
    sbf32 = A("sbf32", [128, 512], F32)

    x_t = A("x_t", [128, XRING * D], F32)
    xnbf = A("xnbf", [128, 4 * D])
    xn2a = [A(f"xn2a{p}", [128, 4096]) for p in range(2)]
    xn2aF = [A(f"xn2aF{p}", [128, 4096]) for p in range(2)]
    stagv = [xn2aF[0].ap().bitcast(F32), xn2aF[1].ap().bitcast(F32)]
    _xdc = A("xdc0", [128, 1024]); xdc = [_xdc, _xdc]
    xdF = [A(f"xdF{p}", [128, 1024]) for p in range(2)]
    _la1 = A("la0", [128, 512], F32); la_s = [_la1, _la1]
    _lx = A("Lx0t", [128, 513], F32); Lx = [_lx, _lx]
    qtF = [A(f"qtF{p}", [128, 2048]) for p in range(2)]
    ktF = [A(f"ktF{p}", [128, 2048]) for p in range(2)]
    kraw = [A(f"kraw{p}", [128, 512]) for p in range(2)]
    ksqF = [A(f"ksqF{p}", [128, 512]) for p in range(2)]
    ktok = [A(f"ktok{p}", [128, 2048]) for p in range(2)]
    vtok = [A(f"vtok{p}", [128, 512]) for p in range(2)]
    vpr = [A(f"vpr{p}", [128, 512]) for p in range(2)]
    _e1s = A("E1s0", [128, 512], F32); E1s = [_e1s, _e1s]
    e1k = [A(f"e1k{p}", [128, 512]) for p in range(2)]
    _ek1 = A("ekarg0", [128, 512], F32); ekarg = [_ek1, _ek1]
    ECt = [A(f"ECt{p}", [128, 16], F32) for p in range(2)]
    _aev = A("Aev0", [128, 512]); Aev = [_aev, _aev]
    _aod = A("Aod0", [128, 512]); Aod = [_aod, _aod]
    osb = [A(f"osb{p}", [128, 512]) for p in range(2)]
    _gsb = A("gsb0", [128, 1024]); gsb = [_gsb, _gsb]
    wsb = [A(f"wsb{p}", [128, 1024]) for p in range(2)]
    outb = [A(f"outb{p}", [128, 1024], F32) for p in range(2)]
    _qav = A("q_av0", [128, 1024], F32); q_av = [_qav, _qav]
    _qsg = A("q_sg0", [128, 1024]); q_sg = [_qsg, _qsg]
    _qt1 = A("q_t10", [128, 1024], F32); q_t1 = [_qt1, _qt1]
    _qt4 = A("q_t40", [128, 1024], F32); q_t4 = [_qt4, _qt4]
    _qt5 = A("q_t50", [128, 1024]); q_t5 = [_qt5, _qt5]
    bmax = [A(f"bmax{p}", [128, 32], F32) for p in range(2)]
    sclf = [A(f"sclf{p}", [128, 32], F32) for p in range(2)]
    sclb = [A(f"sclb{p}", [128, 32]) for p in range(2)]
    rcpb = [A(f"rcpb{p}", [128, 32]) for p in range(2)]
    bn6 = [A(f"bn6{p}", [128, 12], F32) for p in range(2)]
    mv = [A(f"mv{p}", [128, 4], F32) for p in range(2)]
    sml = [A(f"sml{p}", [128, 16], F32) for p in range(4)]
    rkt = [A(f"rkt{p}", [128, 32], F32) for p in range(2)]
    rkf = [A(f"rkf{p}", [128, 8], F32) for p in range(2)]
    nlc = [A(f"nlc{p}", [128, 4], F32) for p in range(2)]

    PBK = [nc.alloc_psum_tensor(f"pb{i}", [128, 512], F32) for i in range(8)]
    pq, pk, pa, pv, pA, pB_, po, pt = PBK
    pA_bf = pA.ap().bitcast(BF16)
    pB_bf = pB_.ap().bitcast(BF16)

    nc.const_aps.register(F32, EPS, c_vals.ap()[:, 0:1])

    S = Sched()
    R = {}
    def res(*key):
        if key not in R:
            R[key] = Res()
        return R[key]

    last_small = {e: (None, None, set()) for e in ENGS}
    def rstep(eng, fn, reads=(), writes=(), extra=(), small=False):
        deps = list(extra)
        for r in reads:
            deps += r.rdeps()
        for r in writes:
            deps += r.wdeps()
        ptok, psmall, pwrites = last_small[eng]
        self_wait = 0
        if psmall and (set(id(r) for r in reads) & pwrites
                       or set(id(r) for r in writes) & pwrites):
            self_wait = ptok[2]
        t = S.step(eng, fn, deps=deps, self_wait=self_wait)
        for r in reads:
            r.read(t)
        for r in writes:
            r.wrote(t)
        last_small[eng] = (t, small, set(id(r) for r in writes))
        return t

    def rdma(eng, fn, reads=(), writes=(), extra=()):
        deps = list(extra)
        for r in reads:
            deps += r.rdeps()
        for r in writes:
            deps += r.wdeps()
        t = S.dma(eng, fn, deps=deps)
        for r in reads:
            r.read(t)
        for r in writes:
            r.wrote(t)
        return t

    rPB = [res("pb", i) for i in range(8)]
    rpq, rpk, rpa, rpv, rpA, rpB, rpo, rpt = rPB
    r_stag = [res("stag", i) for i in range(2)]
    r_x = [res("x", i) for i in range(XRING)]
    r_xnbf = [res("xnbf", i) for i in range(4)]

    t_cv = rdma("sp", lambda e: e.dma_start(out=c_vals[:], in_=cv_d))
    t_wmk = rdma("sp", lambda e: e.dma_start(out=wmkt[:], in_=wmk_d), writes=[res("wmk")])
    t_id = rdma("sp", lambda e: e.dma_start(out=c_ident[:], in_=ident_d))
    t_on = rdma("sp", lambda e: e.dma_start(out=c_ones[:], in_=ones_d))
    t_tri = rdma("sp", lambda e: e.dma_start(out=c_tri4[:], in_=tri4_d))
    t_bd = rdma("sp", lambda e: e.dma_start(out=c_bd[:], in_=bd_d))
    t_gm = rdma("sp", lambda e: e.dma_start(
        out=gamma_pp[:], allow_slow_non_contiguous=True, in_=gamma_d.rearrange("(k p) -> p k", p=128)))
    t_bt = rdma("sp", lambda e: e.dma_start(
        out=beta_pp[:], allow_slow_non_contiguous=True, in_=beta_d.rearrange("(k p) -> p k", p=128)))
    t_ba = rdma("sp", lambda e: e.dma_start(
        out=ba_pp[:], allow_slow_non_contiguous=True, in_=ba_d.rearrange("(g p) -> p g", p=128)))
    t_fb = rdma("pool", lambda e: e.dma_start(
        out=fb_row[:], in_=fb_d.rearrange("(a d) -> a d", a=1)))
    t_szero = rstep("dve", lambda e: [e.memset(sbf[:], 0.0), e.memset(sbf32[:], 0.0)],
                    writes=[res("sbf"), res("sbf32")])
    for p in range(2):
        rstep("dve", lambda e, p=p: e.memset(Lx[p][:, 0:1], 0.0),
              writes=[res("Lx0")])

    wlist = [("q", wq_d, wqb), ("k", wk_d, wkb), ("a", wa_d, wab), ("v", wv_d, wvb)]
    t_bias_mm = []
    for wi, (wn, wdr, wsb_) in enumerate(wlist):
        for kt in range(8):
            sl = (wi * 8 + kt) % 2
            st = stagv[sl][:, 0:512]
            t_ld = rdma("sp", lambda e, wdr=wdr, kt=kt, st=st: e.dma_start(
                out=st, in_=wdr[kt * 128:(kt + 1) * 128, :]),
                writes=[r_stag[sl]], extra=[t_bt if kt == 0 and wi == 0 else None])
            if wn != "v":
                def mk_mm(kt=kt, st=st, wi=wi):
                    def f(e):
                        out = []
                        for fg in range(4):
                            out.append(e.matmul(
                                PBK[fg][:, wi:wi + 1],
                                st[:, fg * 128:(fg + 1) * 128],
                                beta_pp[:, kt:kt + 1],
                                start=(kt == 0), stop=(kt == 7),
                                skip_group_check=True))
                        return out
                    return f
                t_mm = rstep("pe", mk_mm(), reads=[r_stag[sl]],
                             writes=[rPB[fg] for fg in range(4)] if kt == 0 else (),
                             extra=[t_bt] + ([rPB[fg].w for fg in range(4)]
                                             if kt > 0 else []))
            else:
                t_mm = rstep("pe", lambda e, kt=kt, st=st: e.matmul(
                    pt[0:1, :], beta_pp[:, kt:kt + 1], st,
                    start=(kt == 0), stop=(kt == 7)),
                    reads=[r_stag[sl]],
                    writes=[rpt] if kt == 0 else (),
                    extra=[t_bt] + ([rpt.w] if kt > 0 else []))
            t_bias_mm.append(t_mm)
            t_f = rstep("dve", lambda e, kt=kt, st=st, wsb_=wsb_: e.tensor_scalar(
                wsb_[:, kt * 512:(kt + 1) * 512], st,
                gamma_pp[:, kt:kt + 1], None, OP.mult),
                reads=[r_stag[sl]], writes=[res("wfold", wn, kt)],
                extra=[t_gm, t_mm])
    t_bq = rstep("dve", lambda e: [e.tensor_copy(bq_pp[:, fg:fg + 1],
                                                 PBK[fg][:, 0:1]) for fg in range(4)],
                 reads=[rPB[fg] for fg in range(4)], writes=[res("bq")])
    t_bk = rstep("dve", lambda e: [e.tensor_copy(bk_pp[:, fg:fg + 1],
                                                 PBK[fg][:, 1:2]) for fg in range(4)],
                 reads=[rPB[fg] for fg in range(4)], writes=[res("bk")])
    t_bab = rstep("dve", lambda e: [e.tensor_tensor(
        babias_pp[:, fg:fg + 1], PBK[fg][:, 2:3], ba_pp[:, fg:fg + 1], OP.add)
        for fg in range(4)],
        reads=[rPB[fg] for fg in range(4)], writes=[res("bab")], extra=[t_ba])
    t_bv = rstep("dve", lambda e: e.tensor_copy(bv_row[:], pt[0:1, :]),
                 reads=[rpt], writes=[res("bv")])

    for pg in range(4):
        sl = pg % 2
        st = stagv[sl][:, 0:1024]
        t_ld = rdma("sp", lambda e, pg=pg, st=st: e.dma_start(
            out=st, in_=wo_d[pg * 128:(pg + 1) * 128, :]), writes=[r_stag[sl]])
        rstep("dve", lambda e, pg=pg, st=st: e.tensor_copy(
            wob[:, pg * 1024:(pg + 1) * 1024], st),
            reads=[r_stag[sl]], writes=[res("wob", pg)])

    for rt in range(8):
        p = rt % 2
        st = stagv[p][:, 0:1024]
        t_ld = rdma("sp", lambda e, rt=rt, st=st: e.dma_start(
            out=st, in_=wd_d[rt * 128:(rt + 1) * 128, :]), writes=[r_stag[p]])
        t_bm = rstep("dve", lambda e, st=st, p=p: e.reduce_max(
            bmax[p][:], st.rearrange("q (a b) -> q a b", b=32), AX.X,
            apply_absolute_value=True),
            reads=[r_stag[p]], writes=[res("bmax", p)], small=True)
        t_sc = rstep("dve", lambda e, p=p: e.tensor_scalar(
            sclf[p][:], bmax[p][:], 1.0 / 7.5, 1e-30, OP.mult, OP.max),
            reads=[res("bmax", p)], writes=[res("sclf", p)], small=True)
        t_scb = rstep("dve", lambda e, p=p: e.tensor_copy(sclb[p][:], sclf[p][:]),
                      reads=[res("sclf", p)], writes=[res("sclb", p)], small=True)
        t_rc = rstep("dve", lambda e, p=p: e.reciprocal(sclf[p][:], sclf[p][:]),
                     reads=[res("sclf", p)], writes=[res("sclf", p)], extra=[t_scb],
                     small=True)
        t_rcb = rstep("dve", lambda e, p=p: e.tensor_copy(rcpb[p][:], sclf[p][:]),
                      reads=[res("sclf", p)], writes=[res("rcpb", p)], small=True)
        t_v = rstep("dve", lambda e, st=st, p=p: e.tensor_tensor(
            q_t1[p][:].rearrange("q (a b) -> q a b", b=32),
            st.rearrange("q (a b) -> q a b", b=32),
            rcpb[p].ap().rearrange("q (a o) -> q a o", o=1).broadcast_to((128, 32, 32)),
            OP.mult),
            reads=[r_stag[p], res("rcpb", p)], writes=[res("q_t1")])
        t_av = rstep("act", lambda e, p=p: e.activation(q_av[p][:], q_t1[p][:], AF.Abs),
                     reads=[res("q_t1")], writes=[res("q_av")])
        t_sg = rstep("act", lambda e, p=p: e.activation(q_sg[p][:], q_t1[p][:], AF.Sign),
                     reads=[res("q_t1")], writes=[res("q_sg")])
        t_u = rstep("dve", lambda e, p=p: e._custom_dve(
            OPA, out=q_t1[p][:], in0=q_av[p][:], s0=TH1, s1=TH2, imm2=4.0),
            reads=[res("q_av")], writes=[res("q_t1")])
        t_r2 = rstep("dve", lambda e, p=p: e._custom_dve(
            OPB, out=q_t4[p][:], in0=q_t1[p][:], s0=7.5, s1=8.0, imm2=3.5),
            reads=[res("q_t1")], writes=[res("q_t4")])
        t_n = rstep("dve", lambda e, p=p: e._custom_dve(
            OPC, out=q_t5[p][:], in0=q_t4[p][:], in1=q_t1[p][:], s0=1.5, s1=0.5),
            reads=[res("q_t4"), res("q_t1")], writes=[res("q_t5")])
        t_qa = rstep("dve", lambda e, p=p: e._custom_dve(
            OPD, out=q_t4[p][:], in0=q_av[p][:], in1=q_t5[p][:], s0=TH1, s1=TH2,
            imm2=0.125),
            reads=[res("q_av"), res("q_t5")], writes=[res("q_t4")])
        t_sq = rstep("dve", lambda e, p=p: e.tensor_tensor(
            q_t1[p][:], q_t4[p][:], q_sg[p][:], OP.mult),
            reads=[res("q_t4"), res("q_sg")], writes=[res("q_t1")])
        t_xd = rstep("dve", lambda e, p=p: e.tensor_tensor(
            q_t5[p][:].rearrange("q (a b) -> q a b", b=32),
            q_t1[p][:].rearrange("q (a b) -> q a b", b=32),
            sclb[p].ap().rearrange("q (a o) -> q a o", o=1).broadcast_to((128, 32, 32)),
            OP.mult),
            reads=[res("q_t1"), res("sclb", p)], writes=[res("q_t5")])
        pbank_bf, rp = (pA_bf, rpA) if p == 0 else (pB_bf, rpB)
        t_tr = rstep("pe", lambda e, p=p, pbank_bf=pbank_bf: [
            e.transpose(pbank_bf[:, kt * 128:(kt + 1) * 128],
                        q_t5[p][:, kt * 128:(kt + 1) * 128], c_ident[:])
            for kt in range(8)],
            reads=[res("q_t5")], writes=[rp], extra=[t_id])
        rstep("dve", lambda e, rt=rt, pbank_bf=pbank_bf: [
            e.tensor_copy(wdT[:, kt * 1024 + rt * 128: kt * 1024 + (rt + 1) * 128],
                          pbank_bf[:, kt * 128:(kt + 1) * 128])
            for kt in range(8)],
            reads=[rp], writes=[res("wdT", rt)])

    t_wready = [res("wfold", wn, kt).w for wn, _, _ in wlist for kt in range(8)]
    t_wdTready = [res("wdT", rt).w for rt in range(8)]
    t_wobready = [res("wob", pg).w for pg in range(4)]

    xload = {}
    def load_x(g):
        slot = g % XRING
        xload[g] = rdma("sp", lambda e, g=g, slot=slot: e.dma_start(
            out=x_t[:, slot * D:(slot + 1) * D], in_=x_r[g]), writes=[r_x[slot]])
    for g in range(min(4, NCHUNK)):
        load_x(g)

    for sc in range(nsc if stage >= 2 else 0):
        pp = sc % 2
        for ch in range(NCH_SC):
            g = sc * NCH_SC + ch
            cp = g % 2
            xs = x_t[:, (g % XRING) * D:((g % XRING) + 1) * D]
            xnb = xnbf[:, (g % 4) * D:((g % 4) + 1) * D]
            rx = r_x[g % XRING]
            rxn = r_xnbf[g % 4]
            s = sml[g % 4]
            rs = res("sml", g % 4)
            t_bn = rstep("dve", lambda e, cp=cp, xs=xs: [
                e.bn_stats(bn6[cp][:, 0:6], xs[:, 0:512]),
                e.bn_stats(bn6[cp][:, 6:12], xs[:, 512:1024])],
                reads=[rx], writes=[res("bn6", cp)], extra=[xload[g]], small=True)
            t_ag = rstep("dve", lambda e, cp=cp: [
                e.bn_aggr(mv[cp][:, 0:2], bn6[cp][:, 0:6]),
                e.bn_aggr(mv[cp][:, 2:4], bn6[cp][:, 6:12])],
                reads=[res("bn6", cp)], writes=[res("mv", cp)], small=True)
            rmv = res("mv", cp)
            t_c1 = rstep("dve", lambda e, cp=cp, s=s: [
                e.tensor_tensor(s[:, 11:12], mv[cp][:, 0:1], mv[cp][:, 2:3], OP.add),
                e.tensor_tensor(s[:, 13:14], mv[cp][:, 1:2], mv[cp][:, 3:4], OP.add),
                e.tensor_tensor(s[:, 12:13], mv[cp][:, 0:1], mv[cp][:, 2:3],
                                OP.subtract),
                e.tensor_scalar(s[:, 11:12], s[:, 11:12], 0.5, None, OP.mult),
                e.tensor_scalar(s[:, 13:14], s[:, 13:14], 0.5, None, OP.mult),
                e.tensor_scalar(s[:, 14:15], s[:, 12:13], 0.5, None, OP.mult),
                e.tensor_scalar(s[:, 8:9], s[:, 11:12], -1.0, None, OP.mult),
                e.tensor_tensor(s[:, 15:16], s[:, 14:15], s[:, 14:15], OP.mult)],
                reads=[rmv], writes=[rs], small=True)
            t_c2 = rstep("dve", lambda e, s=s: e.tensor_tensor(
                s[:, 12:13], s[:, 13:14], s[:, 15:16], OP.add),
                reads=[rs], writes=[rs], small=True)
            t_c3 = rstep("dve", lambda e, s=s: e.tensor_scalar(
                s[:, 2:3], s[:, 12:13], EPS, None, OP.add),
                reads=[rs], writes=[rs], small=True)
            t_s1 = rstep("act", lambda e, s=s: e.activation(
                s[:, 0:1], s[:, 12:13], AF.Sqrt, bias=EPS),
                reads=[rs], writes=[rs], extra=[t_cv])
            t_s4 = rstep("dve", lambda e, s=s: e.reciprocal(s[:, 3:4], s[:, 2:3]),
                         reads=[rs], writes=[rs], small=True)
            t_s5 = rstep("dve", lambda e, s=s: e.reciprocal(s[:, 1:2], s[:, 0:1]),
                         reads=[rs], writes=[rs], small=True)
            t_s6 = rstep("dve", lambda e, s=s: e.tensor_tensor(
                s[:, 4:5], s[:, 12:13], s[:, 3:4], OP.mult),
                reads=[rs], writes=[rs], small=True)
            t_s2 = rstep("act", lambda e, s=s: e.activation(
                s[:, 5:6], s[:, 4:5], AF.Sqrt, bias=EPS),
                reads=[rs], writes=[rs], extra=[t_cv])
            t_s7 = rstep("dve", lambda e, s=s: e.reciprocal(s[:, 6:7], s[:, 5:6]),
                         reads=[rs], writes=[rs], small=True)
            t_s8 = rstep("dve", lambda e, s=s: [
                e.tensor_tensor(s[:, 7:8], s[:, 1:2], s[:, 6:7], OP.mult),
                e.tensor_tensor(s[:, 9:10], s[:, 8:9], s[:, 1:2], OP.mult),
                e.tensor_tensor(s[:, 10:11], s[:, 8:9], s[:, 7:8], OP.mult)],
                reads=[rs], writes=[rs], small=True)
            t_xn = rstep("act", lambda e, xs=xs, xnb=xnb, s=s: e.activation(
                xnb, xs, AF.Identity, bias=s[:, 9:10], scale=s[:, 1:2]),
                reads=[rx, rs], writes=[rxn])
            t_x2 = rstep("act", lambda e, xs=xs, s=s, pp=pp, ch=ch: e.activation(
                xn2a[pp][:, ch * D:(ch + 1) * D], xs, AF.Identity,
                bias=s[:, 10:11], scale=s[:, 7:8]),
                reads=[rx, rs], writes=[res("xn2a", pp, ch)])
        if stage < 3 or sc >= sclim:
            continue
        for kt in range(8):
            pbank_bf, rp = (pA_bf, rpA) if kt % 2 == 0 else (pB_bf, rpB)
            t_tr = rstep("pe", lambda e, kt=kt, pbank_bf=pbank_bf, pp=pp: [
                e.transpose(pbank_bf[:, ch * 128:(ch + 1) * 128],
                            xn2a[pp][:, ch * D + kt * 128: ch * D + (kt + 1) * 128],
                            c_ident[:])
                for ch in range(NCH_SC)],
                reads=[res("xn2a", pp, ch) for ch in range(NCH_SC)],
                writes=[rp], extra=[t_id])
            rstep("dve", lambda e, kt=kt, pbank_bf=pbank_bf, pp=pp: e.tensor_copy(
                xn2aF[pp][:, kt * 512:(kt + 1) * 512], pbank_bf[:, 0:512]),
                reads=[rp], writes=[res("xn2aF", pp, kt)],
                extra=(r_stag[pp].wdeps() if sc < 2 else []))
        rxf = [res("xn2aF", pp, kt) for kt in range(8)]

        for fg in range(4 if sub >= 1 else 0):
            t_pz = rstep("pe", lambda e, fg=fg, pp=pp: [
                e.matmul(pa[:], wab[:, kt * 512 + fg * 128: kt * 512 + (fg + 1) * 128],
                         xn2aF[pp][:, kt * 512:(kt + 1) * 512],
                         start=(kt == 0), stop=(kt == 7))
                for kt in range(8)],
                reads=rxf, writes=[rpa], extra=t_wready)
            if sub < 2:
                continue
            t_sig = rstep("act", lambda e, fg=fg, pp=pp: e.activation(
                E1s[pp][:], pa[:], AF.Sigmoid, bias=babias_pp[:, fg:fg + 1]),
                reads=[rpa, res("bab")], writes=[res("E1s")], extra=[t_bab])
            t_la = rstep("act", lambda e, pp=pp: e.activation(
                la_s[pp][:], E1s[pp][:], AF.Ln),
                reads=[res("E1s")], writes=[res("la")])
            t_scan = rstep("dve", lambda e, pp=pp: e.tensor_tensor_scan(
                Lx[pp][:, 1:513], la_s[pp][:], la_s[pp][:], 0.0, OP.add, OP.bypass),
                reads=[res("la"), res("Lx0")], writes=[res("Lx")])
            if sub < 3:
                continue
            t_pq = rstep("pe", lambda e, fg=fg, pp=pp: [
                e.matmul(pq[:], wqb[:, kt * 512 + fg * 128: kt * 512 + (fg + 1) * 128],
                         xn2aF[pp][:, kt * 512:(kt + 1) * 512],
                         start=(kt == 0), stop=(kt == 7))
                for kt in range(8)],
                reads=rxf, writes=[rpq], extra=t_wready)
            t_pk = rstep("pe", lambda e, fg=fg, pp=pp: [
                e.matmul(pk[:], wkb[:, kt * 512 + fg * 128: kt * 512 + (fg + 1) * 128],
                         xn2aF[pp][:, kt * 512:(kt + 1) * 512],
                         start=(kt == 0), stop=(kt == 7))
                for kt in range(8)],
                reads=rxf, writes=[rpk], extra=t_wready)
            t_kraw = rstep("dve", lambda e, fg=fg, pp=pp: e.tensor_scalar(
                kraw[pp][:], pk[:], bk_pp[:, fg:fg + 1], None, OP.add),
                reads=[rpk, res("bk")], writes=[res("kraw", pp)], extra=[t_bk])
            t_ksq = rstep("act", lambda e, pp=pp: e.activation(
                ksqF[pp][:], kraw[pp][:], AF.Square),
                reads=[res("kraw", pp)], writes=[res("ksqF", pp)])
            for ch in range(NCH_SC if sub >= 4 else 0):
                cs = ch * 128
                t_nl = rstep("dve", lambda e, pp=pp, ch=ch, cs=cs: e.tensor_scalar(
                    nlc[pp][:, ch:ch + 1], Lx[pp][:, cs:cs + 1], -1.0, None, OP.mult),
                    reads=[res("Lx")], writes=[res("nlc", pp)], small=True)
                t_e1 = rstep("act", lambda e, pp=pp, ch=ch, cs=cs: e.activation(
                    E1s[pp][:, cs:cs + 128], Lx[pp][:, cs + 1:cs + 129], AF.Exp,
                    bias=nlc[pp][:, ch:ch + 1]),
                    reads=[res("Lx"), res("nlc", pp)], writes=[res("E1s")])
                t_m1 = rstep("dve", lambda e, pp=pp, cs=cs: e.tensor_scalar(
                    ekarg[pp][:, cs:cs + 128], Lx[pp][:, cs + 1:cs + 129], -1.0,
                    Lx[pp][:, cs:cs + 1], OP.mult, OP.add),
                    reads=[res("Lx")], writes=[res("ekarg")])
                t_m2 = rstep("dve", lambda e, pp=pp, cs=cs: e.tensor_scalar(
                    ekarg[pp][:, cs:cs + 128], ekarg[pp][:, cs:cs + 128], 80.0, None,
                    OP.min),
                    reads=[res("ekarg")], writes=[res("ekarg")])
                t_ek = rstep("act", lambda e, pp=pp, cs=cs: e.activation(
                    e1k[pp][:, cs:cs + 128], ekarg[pp][:, cs:cs + 128], AF.Exp),
                    reads=[res("ekarg")], writes=[res("e1k", pp)])
                rstep("dve", lambda e, pp=pp, fg=fg, cs=cs: e.scalar_tensor_tensor(
                    qtF[pp][:, fg * 512 + cs: fg * 512 + cs + 128],
                    pq[:, cs:cs + 128], bq_pp[:, fg:fg + 1], E1s[pp][:, cs:cs + 128],
                    OP.add, OP.mult),
                    reads=[rpq, res("E1s"), res("bq")],
                    writes=[res("qtF", pp, fg, ch)], extra=[t_bq])
                rstep("dve", lambda e, pp=pp, fg=fg, cs=cs: e.tensor_tensor(
                    ktF[pp][:, fg * 512 + cs: fg * 512 + cs + 128],
                    kraw[pp][:, cs:cs + 128], e1k[pp][:, cs:cs + 128], OP.mult),
                    reads=[res("kraw", pp), res("e1k", pp)],
                    writes=[res("ktF", pp, fg, ch)])
                if sub < 5:
                    continue
                kb_bf, rkb = (pA_bf, rpA) if fg % 2 == 0 else (pB_bf, rpB)
                qb_bf, rqb = (pB_bf, rpB) if fg % 2 == 0 else (pA_bf, rpA)
                t_tr = rstep("pe", lambda e, pp=pp, fg=fg, cs=cs, kb_bf=kb_bf,
                             qb_bf=qb_bf: [
                    e.transpose(kb_bf[:, 0:128],
                                ktF[pp][:, fg * 512 + cs: fg * 512 + cs + 128],
                                c_ident[:]),
                    e.transpose(qb_bf[:, 0:128],
                                ksqF[pp][:, cs:cs + 128], c_ident[:])],
                    reads=[res("ktF", pp, fg, ch), res("ksqF", pp)],
                    writes=[rkb, rqb], extra=[t_id])
                rstep("act", lambda e, pp=pp, fg=fg, ch=ch, kb_bf=kb_bf:
                      e.activation(
                          ktok[pp][:, ch * 512 + fg * 128: ch * 512 + (fg + 1) * 128],
                          kb_bf[:, 0:128], AF.Copy),
                      reads=[rkb], writes=[res("ktok", pp, fg, ch)])
                rstep("dve", lambda e, pp=pp, fg=fg, ch=ch, qb_bf=qb_bf:
                      e.tensor_reduce(
                          rkt[pp][:, ch * 8 + fg * 2: ch * 8 + (fg + 1) * 2],
                          qb_bf[:, 0:128].rearrange("q (a b) -> q a b", b=64),
                          AX.X, OP.add),
                      reads=[rqb], writes=[res("rkt", pp, fg, ch)], small=True)
            if sub < 6:
                continue
            rstep("dve", lambda e, pp=pp, fg=fg: e.tensor_copy(
                ECt[pp][:, fg * 4:(fg + 1) * 4],
                AP(E1s[pp].ap().tensor, E1s[pp].ap().offset + 127, [[512, 128], [128, 4]])),
                reads=[res("E1s")], writes=[res("ECt", pp, fg)], small=True)

        if stage < 4:
            continue
        for ch in range(min(NCH_SC, chlim)):
            g = sc * NCH_SC + ch
            cp = g % 2
            cs = ch * 128
            xs = x_t[:, (g % XRING) * D:((g % XRING) + 1) * D]
            xnb = xnbf[:, (g % 4) * D:((g % 4) + 1) * D]
            rx = r_x[g % XRING]
            rxn = r_xnbf[g % 4]
            s4 = sml[g % 4]
            rs4 = res("sml", g % 4)
            t_xf = rstep("act", lambda e, cp=cp, xs=xs, s4=s4: e.activation(
                outb[cp][:], xs, AF.Identity, bias=s4[:, 9:10], scale=s4[:, 1:2]),
                reads=[rx, rs4], writes=[res("outb", cp)])
            t_pv = rstep("pe", lambda e, pp=pp, cs=cs: [
                e.matmul(pv[:],
                         xn2aF[pp][:, kt * 512 + cs: kt * 512 + cs + 128],
                         wvb[:, kt * 512:(kt + 1) * 512],
                         start=(kt == 0), stop=False)
                for kt in range(8)] + [
                e.matmul(pv[:], c_ones[:], bv_row[:], start=False, stop=True)],
                reads=rxf + [res("bv")], writes=[rpv],
                extra=t_wready + [t_on, t_bv])
            t_vt = rstep("act", lambda e, cp=cp: e.activation(
                vtok[cp][:], pv[:], AF.Copy),
                reads=[rpv], writes=[res("vtok", cp)])
            t_rk1 = rstep("act", lambda e, pp=pp, cp=cp, ch=ch: e.activation(
                rkf[cp][:], rkt[pp][:, ch * 8: ch * 8 + 8], AF.Sqrt),
                reads=[res("rkt", pp, fgx, ch) for fgx in range(4)],
                writes=[res("rkf", cp)])
            t_rk2 = rstep("dve", lambda e, cp=cp: e.tensor_scalar(
                rkf[cp][:], rkf[cp][:], 1e-12, None, OP.max),
                reads=[res("rkf", cp)], writes=[res("rkf", cp)], small=True)
            t_rk3 = rstep("dve", lambda e, cp=cp: e.reciprocal(rkf[cp][:], rkf[cp][:]),
                          reads=[res("rkf", cp)], writes=[res("rkf", cp)],
                          extra=[t_rk2], small=True)
            t_vp = rstep("dve", lambda e, cp=cp: [
                e.tensor_scalar(vpr[cp][:, h * 64:(h + 1) * 64],
                                vtok[cp][:, h * 64:(h + 1) * 64],
                                rkf[cp][:, h:h + 1], None, OP.mult)
                for h in range(8)],
                reads=[res("vtok", cp), res("rkf", cp)], writes=[res("vpr", cp)])
            if g < WARM:
                rstep("dve", lambda e, cp=cp: e.tensor_scalar(
                    vpr[cp][:], vpr[cp][:], wmkt[:, 0:1], None, OP.mult),
                    reads=[res("vpr", cp), res("wmk")], writes=[res("vpr", cp)],
                    extra=[t_wmk])
            t_bm = rstep("dve", lambda e, cp=cp: e.reduce_max(
                bmax[cp][:], outb[cp][:].rearrange("q (a b) -> q a b", b=32), AX.X,
                apply_absolute_value=True),
                reads=[res("outb", cp)], writes=[res("bmax", cp)], small=True)
            t_sc_ = rstep("dve", lambda e, cp=cp: e.tensor_scalar(
                sclf[cp][:], bmax[cp][:], 1.0 / 7.5, 1e-30, OP.mult, OP.max),
                reads=[res("bmax", cp)], writes=[res("sclf", cp)], small=True)
            t_scb = rstep("dve", lambda e, cp=cp: e.tensor_copy(sclb[cp][:], sclf[cp][:]),
                          reads=[res("sclf", cp)], writes=[res("sclb", cp)], small=True)
            t_rc = rstep("dve", lambda e, cp=cp: e.reciprocal(sclf[cp][:], sclf[cp][:]),
                         reads=[res("sclf", cp)], writes=[res("sclf", cp)],
                         extra=[t_scb], small=True)
            t_rcb = rstep("dve", lambda e, cp=cp: e.tensor_copy(rcpb[cp][:], sclf[cp][:]),
                          reads=[res("sclf", cp)], writes=[res("rcpb", cp)], small=True)
            t_v = rstep("dve", lambda e, cp=cp: e.tensor_tensor(
                q_t1[cp][:].rearrange("q (a b) -> q a b", b=32),
                outb[cp][:].rearrange("q (a b) -> q a b", b=32),
                rcpb[cp].ap().rearrange("q (a o) -> q a o", o=1).broadcast_to((128, 32, 32)),
                OP.mult),
                reads=[res("outb", cp), res("rcpb", cp)], writes=[res("q_t1")])
            t_av = rstep("act", lambda e, cp=cp: e.activation(
                q_av[cp][:], q_t1[cp][:], AF.Abs),
                reads=[res("q_t1")], writes=[res("q_av")])
            t_sg = rstep("act", lambda e, cp=cp: e.activation(
                q_sg[cp][:], q_t1[cp][:], AF.Sign),
                reads=[res("q_t1")], writes=[res("q_sg")])
            t_u = rstep("dve", lambda e, cp=cp: e._custom_dve(
                OPA, out=q_t1[cp][:], in0=q_av[cp][:], s0=TH1, s1=TH2, imm2=4.0),
                reads=[res("q_av")], writes=[res("q_t1")])
            t_r2q = rstep("dve", lambda e, cp=cp: e._custom_dve(
                OPB, out=q_t4[cp][:], in0=q_t1[cp][:], s0=7.5, s1=8.0, imm2=3.5),
                reads=[res("q_t1")], writes=[res("q_t4")])
            t_n = rstep("dve", lambda e, cp=cp: e._custom_dve(
                OPC, out=q_t5[cp][:], in0=q_t4[cp][:], in1=q_t1[cp][:], s0=1.5, s1=0.5),
                reads=[res("q_t4"), res("q_t1")], writes=[res("q_t5")])
            t_qa = rstep("dve", lambda e, cp=cp: e._custom_dve(
                OPD, out=q_t4[cp][:], in0=q_av[cp][:], in1=q_t5[cp][:], s0=TH1, s1=TH2,
                imm2=0.125),
                reads=[res("q_av"), res("q_t5")], writes=[res("q_t4")])
            t_sq = rstep("dve", lambda e, cp=cp: e.tensor_tensor(
                q_t1[cp][:], q_t4[cp][:], q_sg[cp][:], OP.mult),
                reads=[res("q_t4"), res("q_sg")], writes=[res("q_t1")])
            t_xd = rstep("dve", lambda e, cp=cp: e.tensor_tensor(
                xdc[cp][:].rearrange("q (a b) -> q a b", b=32),
                q_t1[cp][:].rearrange("q (a b) -> q a b", b=32),
                sclb[cp].ap().rearrange("q (a o) -> q a o", o=1).broadcast_to((128, 32, 32)),
                OP.mult),
                reads=[res("q_t1"), res("sclb", cp)], writes=[res("xdc")])

            def mk_A(pp=pp, cs=cs):
                def f(e):
                    out = []
                    for fgx in range(4):
                        out.append(e.matmul(
                            pA[:, fgx * 128:(fgx + 1) * 128],
                            ktF[pp][0:64, fgx * 512 + cs: fgx * 512 + cs + 128],
                            qtF[pp][0:64, fgx * 512 + cs: fgx * 512 + cs + 128],
                            start=True, stop=True, skip_group_check=True))
                        out.append(e.matmul(
                            pB_[:, fgx * 128:(fgx + 1) * 128],
                            ktF[pp][64:128, fgx * 512 + cs: fgx * 512 + cs + 128],
                            qtF[pp][64:128, fgx * 512 + cs: fgx * 512 + cs + 128],
                            start=True, stop=True, skip_group_check=True))
                    return out
                return f
            t_Ae = rstep("pe", mk_A(),
                reads=[res("ktF", pp, fgx, ch) for fgx in range(4)] +
                      [res("qtF", pp, fgx, ch) for fgx in range(4)],
                writes=[rpA, rpB])
            t_Ao = t_Ae
            t_Aev = rstep("dve", lambda e, cp=cp: e.tensor_tensor(
                Aev[cp][:], pA[:], c_tri4[:], OP.mult),
                reads=[rpA], writes=[res("Aev")], extra=[t_tri])
            t_Aod = rstep("dve", lambda e, cp=cp: e.tensor_tensor(
                Aod[cp][:], pB_[:], c_tri4[:], OP.mult),
                reads=[rpB], writes=[res("Aod")], extra=[t_tri])
            def mk_o(pp=pp, cp=cp, cs=cs):
                def f(e):
                    out = []
                    for pg in range(4):
                        out.append(e.matmul(
                            po[:, pg * 128:(pg + 1) * 128],
                            sbf[:, pg * 128:(pg + 1) * 128],
                            qtF[pp][:, pg * 512 + cs: pg * 512 + cs + 128],
                            start=True, stop=False, skip_group_check=True))
                        out.append(e.matmul(
                            po[0:64, pg * 128:(pg + 1) * 128],
                            vpr[cp][:, pg * 128: pg * 128 + 64],
                            Aev[cp][:, pg * 128:(pg + 1) * 128],
                            start=False, stop=False, skip_group_check=True))
                        out.append(e.matmul(
                            po[64:128, pg * 128:(pg + 1) * 128],
                            vpr[cp][:, pg * 128 + 64: pg * 128 + 128],
                            Aod[cp][:, pg * 128:(pg + 1) * 128],
                            start=False, stop=True, skip_group_check=True))
                    return out
                return f
            t_o = rstep("pe", mk_o(),
                        reads=[res("sbf"), res("vpr", cp), res("Aev"),
                               res("Aod")] +
                              [res("qtF", pp, fgx, ch) for fgx in range(4)],
                        writes=[rpo])
            t_T0 = rstep("pe", lambda e, pp=pp, cp=cp, ch=ch: [
                e.matmul(pt[:, pg * 128:(pg + 1) * 128],
                         ktok[pp][:, ch * 512 + pg * 128: ch * 512 + (pg + 1) * 128],
                         vpr[cp][:, pg * 128:(pg + 1) * 128],
                         start=True, stop=True, skip_group_check=True)
                for pg in range(4)],
                reads=[res("ktok", pp, fgx, ch) for fgx in range(4)] +
                      [res("vpr", cp)],
                writes=[rpt])
            t_su = rstep("dve", lambda e: e.tensor_tensor(
                sbf32[:], sbf32[:], pt[:], OP.add),
                reads=[res("sbf32"), rpt], writes=[res("sbf32")], extra=[t_o])
            t_s2_ = rstep("dve", lambda e, cp=cp, pp=pp, ch=ch: [
                e.scalar_tensor_tensor(
                    sbf32[:, pg * 128:(pg + 1) * 128],
                    sbf32[:, pg * 128:(pg + 1) * 128],
                    ECt[pp][:, pg * 4 + ch: pg * 4 + ch + 1],
                    c_bd[:], OP.mult, OP.mult)
                for pg in range(4)],
                reads=[res("sbf32")] + [res("ECt", pp, fgx) for fgx in range(4)],
                writes=[res("sbf32")], extra=[t_bd])
            t_scast = rstep("act", lambda e: e.activation(
                sbf[:], sbf32[:], AF.Copy),
                reads=[res("sbf32")], writes=[res("sbf")])
            t_oe = rstep("act", lambda e, cp=cp: e.activation(
                osb[cp][:], po[:], AF.Copy),
                reads=[rpo], writes=[res("osb", cp)])
            for half in range(2):
                pbank_bf, rp = (pA_bf, rpA) if half == 0 else (pB_bf, rpB)
                t_tr = rstep("pe", lambda e, cp=cp, half=half, pbank_bf=pbank_bf: [
                    e.transpose(pbank_bf[:, kk * 128:(kk + 1) * 128],
                                xdc[cp][:, (half * 4 + kk) * 128:
                                         (half * 4 + kk + 1) * 128],
                                c_ident[:])
                    for kk in range(4)],
                    reads=[res("xdc")], writes=[rp], extra=[t_id])
                rstep("dve", lambda e, cp=cp, half=half, pbank_bf=pbank_bf:
                      e.tensor_copy(xdF[cp][:, half * 512:(half + 1) * 512],
                                    pbank_bf[:, 0:512]),
                      reads=[rp], writes=[res("xdF", cp, half)])
            def mk_wo(mhalf, bank, cp=cp, xnb=xnb):
                def f(e):
                    out = [e.matmul(bank[:], c_ident[:],
                                    xnb[:, mhalf * 512:(mhalf + 1) * 512],
                                    start=True, stop=False)]
                    for pg in range(4):
                        out.append(e.matmul(
                            bank[:],
                            osb[cp][:, pg * 128:(pg + 1) * 128],
                            wob[:, pg * 1024 + mhalf * 512:
                                pg * 1024 + (mhalf + 1) * 512],
                            start=False, stop=(pg == 3)))
                    return out
                return f
            if stage < 5:
                continue
            t_wo0 = rstep("pe", mk_wo(0, pq), reads=[res("osb", cp), rxn],
                          writes=[rpq], extra=t_wobready + [t_id])
            t_wo1 = rstep("pe", mk_wo(1, pk), reads=[res("osb", cp), rxn],
                          writes=[rpk], extra=t_wobready + [t_id])
            def mk_ffn(mhalf, bank, cp=cp):
                def f(e):
                    out = []
                    for kt in range(8):
                        out.append(e.matmul(
                            bank[:],
                            xdF[cp][:, kt * 128:(kt + 1) * 128],
                            wdT[:, kt * 1024 + mhalf * 512:
                                kt * 1024 + (mhalf + 1) * 512],
                            start=(kt == 0), stop=False))
                    out.append(e.matmul(bank[:], c_ones[:],
                                        fb_row[:, mhalf * 512:(mhalf + 1) * 512],
                                        start=False, stop=True))
                    return out
                return f
            t_ff0 = rstep("pe", mk_ffn(0, pa),
                          reads=[res("xdF", cp, 0), res("xdF", cp, 1)],
                          writes=[rpa], extra=t_wdTready + [t_on, t_fb])
            t_ff1 = rstep("pe", mk_ffn(1, pv),
                          reads=[res("xdF", cp, 0), res("xdF", cp, 1)],
                          writes=[rpv], extra=t_wdTready + [t_on, t_fb])
            t_g0 = rstep("act", lambda e, cp=cp: e.activation(
                gsb[cp][:, 0:512], pa[:], AF.Silu), reads=[rpa],
                writes=[res("gsb")])
            t_g1 = rstep("act", lambda e, cp=cp: e.activation(
                gsb[cp][:, 512:1024], pv[:], AF.Silu), reads=[rpv],
                writes=[res("gsb")])
            t_w0 = rstep("dve", lambda e, cp=cp: e.tensor_tensor(
                wsb[cp][:, 0:512], pq[:], gsb[cp][:, 0:512], OP.mult),
                reads=[rpq, res("gsb")], writes=[res("wsb", cp)],
                extra=[t_g0])
            t_w1 = rstep("dve", lambda e, cp=cp: e.tensor_tensor(
                wsb[cp][:, 512:1024], pk[:], gsb[cp][:, 512:1024], OP.mult),
                reads=[rpk, res("gsb")], writes=[res("wsb", cp)],
                extra=[t_g1])
            if g >= WARM:
                t_out = rstep("pool", lambda e, cp=cp, xs=xs: e.tensor_tensor(
                    outb[cp][:], wsb[cp][:], xs, OP.add),
                    reads=[res("wsb", cp), rx], writes=[res("outb", cp)],
                    extra=[t_w0, t_w1])
                rdma("sp", lambda e, cp=cp, g=g: e.dma_start(
                    out=out_r[g - WARM], in_=outb[cp][:]),
                    reads=[res("outb", cp)])
            if g + 4 < NCHUNK:
                load_x(g + 4)

    ltab = dict(locals())
    for dn in dumps:
        t = ltab[dn]
        if isinstance(t, list):
            t = t[0]
        shape = list(t.ap().shape)
        dt_ = t.ap().dtype
        dd = nc.dram_tensor("dump_" + dn, shape, dt_, kind="ExternalOutput").ap()
        rdma("sp", lambda e, dd=dd, t=t: e.dma_start(out=dd, in_=t[:]),
             extra=[("s", "dve", S.cnt["dve"]), ("s", "act", S.cnt["act"]),
                    ("s", "pe", S.cnt["pe"]), ("s", "pool", S.cnt["pool"])])

    total_sp = S.dcnt["sp"]
    S.step("pool", lambda e: e.engine_nop(), deps=[("d", "sp", total_sp)])

    with (
        nc.semaphore("s_sp") as s_sp, nc.semaphore("s_act") as s_act,
        nc.semaphore("s_pe") as s_pe, nc.semaphore("s_dve") as s_dve,
        nc.semaphore("s_pool") as s_pool,
        nc.semaphore("d_sp") as d_sp, nc.semaphore("d_act") as d_act,
        nc.semaphore("d_pool") as d_pool,
        nc.Block() as block,
    ):
        sems = {"sp": s_sp, "act": s_act, "pe": s_pe, "dve": s_dve, "pool": s_pool}
        dsems = {"sp": d_sp, "act": d_act, "pool": d_pool}
        S.emit(block, sems, dsems)

    mybir.codegen_inst_isa_subclasses(nc)
    return nc


_NC_CACHE = {}

def kernel(L0_attn_x, L0_attn_Wq, L0_attn_Wk, L0_attn_Wv, L0_attn_Wa, L0_attn_ba,
           L0_attn_Wo, L0_attn_gamma, L0_attn_beta, L0_ffn_weight, L0_ffn_bias):
    if "nc" not in _NC_CACHE:
        _NC_CACHE["nc"] = build_program()
    nc = _NC_CACHE["nc"]

    x = np.ascontiguousarray(np.asarray(L0_attn_x, dtype=np.float32))
    B, T, Dm = x.shape
    nseg = 4
    xs = x.reshape(B, nseg, SEG, Dm)

    ident = np.eye(128, dtype=ml_dtypes.bfloat16)
    ones = np.ones((1, 128), dtype=ml_dtypes.bfloat16)
    tri = np.triu(np.ones((128, 128), dtype=np.float32)).astype(ml_dtypes.bfloat16)
    tri4 = np.tile(tri, (1, 4))
    bd = np.zeros((128, 128), dtype=np.float32)
    bd[:64, :64] = 1.0
    bd[64:, 64:] = 1.0
    bd = bd.astype(ml_dtypes.bfloat16)
    cvals = np.zeros((128, 4), dtype=np.float32)
    cvals[:, 0] = EPS

    common = dict(
        wq=np.ascontiguousarray(np.asarray(L0_attn_Wq, np.float32)),
        wk=np.ascontiguousarray(np.asarray(L0_attn_Wk, np.float32)),
        wv=np.ascontiguousarray(np.asarray(L0_attn_Wv, np.float32)),
        wa=np.ascontiguousarray(np.asarray(L0_attn_Wa, np.float32)),
        ba=np.ascontiguousarray(np.asarray(L0_attn_ba, np.float32)),
        wo=np.ascontiguousarray(np.asarray(L0_attn_Wo, np.float32)),
        gamma=np.ascontiguousarray(np.asarray(L0_attn_gamma, np.float32)),
        beta=np.ascontiguousarray(np.asarray(L0_attn_beta, np.float32)),
        wd=np.ascontiguousarray(np.asarray(L0_ffn_weight, np.float32)),
        fbias=np.ascontiguousarray(np.asarray(L0_ffn_bias, np.float32)),
        c_ident=ident, c_ones=ones, c_tri4=tri4, c_bd=bd, c_vals=cvals,
    )
    halo = WARM * CH
    in_maps = []
    for c in range(8):
        b, s = c // nseg, c % nseg
        m = dict(common)
        prev = xs[b, s - 1][-halo:] if s > 0 else xs[b, 0][:halo]
        m["x_in"] = np.ascontiguousarray(np.concatenate([prev, xs[b, s]], axis=0))
        m["wmk"] = np.full((128, 1), 0.0 if s == 0 else 1.0, np.float32)
        in_maps.append(m)

    res = run_bass_kernel_spmd(nc, in_maps, core_ids=list(range(8)))
    out = np.empty((B, nseg, SEG, Dm), dtype=np.float32)
    for c in range(8):
        b, s = c // nseg, c % nseg
        out[b, s] = np.asarray(res.results[c]["out"], dtype=np.float32)
    return out.reshape(B, T, Dm)
